# version 25
# baseline (speedup 1.0000x reference)
import numpy as np
import ml_dtypes
from contextlib import ExitStack

import concourse.bass as bass
import concourse.mybir as mybir
from concourse.bass_utils import run_bass_kernel_spmd

F32 = mybir.dt.float32
BF16 = mybir.dt.bfloat16
I32 = mybir.dt.int32
AF = mybir.ActivationFunctionType
ALU = mybir.AluOpType
AX = mybir.AxisListType

T_TOK = 2048
D = 1024
NT = 16
SCALE = float(128) ** -0.5
EPS = 1e-5
N_CORES = 8
CVALS = [0.5, EPS, -0.5, -1.0, SCALE, 1.0 / D, 16.5, 0.0]

DMA_SEMS = {"ld", "l2", "l3", "l4", "l5", "gd", "g2", "g3", "g4", "g5", "st"}
ENG_SEM = {"tensor": "t", "vector": "v", "scalar": "a", "gpsimd": "p"}


class Tok:
    __slots__ = ("sem", "cnt")

    def __init__(self, sem, cnt):
        self.sem, self.cnt = sem, cnt


ENG_DUR = {"vector": 0.9, "scalar": 1.2, "tensor": 1.0, "gpsimd": 3.0, "sync": 3.0}


class Sched:
    def __init__(self):
        self.entries = []
        self.cnt = {}
        self.buf_free = {}
        self.const_tok = None

    def add(self, eng, fn, deps=(), inc=None, inc_n=1, dur=None, tag=None):
        flat = []
        for d in deps:
            if d is None:
                continue
            if isinstance(d, list):
                flat.extend(x for x in d if x is not None)
            else:
                flat.append(d)
        if eng == "scalar" and self.const_tok is not None:
            flat.append(self.const_tok)
        deps = flat
        incs = []
        tok = None
        if inc is not None:
            base = self.cnt.get(inc, 0)
            self.cnt[inc] = base + inc_n
            tok = Tok(inc, self.cnt[inc])
            incs.append((inc, inc_n, base))
        self.entries.append([eng, fn, deps, incs, tok, dur, tag])
        return tok

    def bfree(self, buf):
        return list(self.buf_free.get(id(buf), []))

    def buse(self, buf, tok):
        self.buf_free.setdefault(id(buf), []).append(tok)

    def reorder(self):
        ents = self.entries
        n = len(ents)
        prod = {}
        for i, ent in enumerate(ents):
            if ent[4] is not None:
                prod[id(ent[4])] = i
        preds = [set() for _ in range(n)]
        last_t = None
        for i, ent in enumerate(ents):
            for d in ent[2]:
                j = prod.get(id(d))
                if j is not None and j != i:
                    preds[i].add(j)
            if ent[0] == "tensor":
                if last_t is not None:
                    preds[i].add(last_t)
                last_t = i
        succs = [[] for _ in range(n)]
        for i in range(n):
            for j in preds[i]:
                succs[j].append(i)
        dur = [ent[5] if ent[5] is not None else ENG_DUR[ent[0]] for ent in ents]
        prio = [0.0] * n
        for i in range(n - 1, -1, -1):
            m = 0.0
            for s in succs[i]:
                if prio[s] > m:
                    m = prio[s]
            prio[i] = dur[i] + m
        indeg = [len(preds[i]) for i in range(n)]
        finish = [0.0] * n
        eng_free = {}
        sem_busy = {}
        ready = [i for i in range(n) if indeg[i] == 0]
        order = []
        cur_tab = "nat"
        while ready:
            best = None
            for i in ready:
                ent = ents[i]
                st = eng_free.get(ent[0], 0.0)
                for j in preds[i]:
                    if finish[j] > st:
                        st = finish[j]
                isdma = bool(ent[3]) and ent[3][0][0] in DMA_SEMS
                if isdma:
                    sb_ = sem_busy.get(ent[3][0][0], 0.0)
                    if sb_ > st:
                        st = sb_
                pen = 0.0
                if ent[0] == "scalar" and (ent[6] or "nat") != cur_tab:
                    pen = 1.3
                key = (int((st + pen) / 4.0), -prio[i], i)
                if best is None or key < best[0]:
                    best = (key, i, st + pen, isdma)
            _, i, st, isdma = best
            ready.remove(i)
            ent = ents[i]
            finish[i] = st + dur[i]
            if isdma:
                sem_busy[ent[3][0][0]] = finish[i]
                eng_free[ent[0]] = st + 0.05
            else:
                eng_free[ent[0]] = finish[i]
                if ent[0] == "scalar":
                    cur_tab = ent[6] or "nat"
            order.append(i)
            for s in succs[i]:
                indeg[s] -= 1
                if indeg[s] == 0:
                    ready.append(s)
        assert len(order) == n, "cycle in schedule graph"
        self.entries = [ents[i] for i in order]
        cnt = {}
        for ent in self.entries:
            newincs = []
            for (sem, nn, _b) in ent[3]:
                b = cnt.get(sem, 0)
                cnt[sem] = b + nn
                newincs.append((sem, nn, b))
                if ent[4] is not None and ent[4].sem == sem:
                    ent[4].cnt = b + nn
            ent[3][:] = newincs
        return max(finish) if n else 0.0


def build_nc(debug=False, dbg_tau=0):
    nc = bass.Bass("TRN2", target_bir_lowering=False)

    x_h = nc.dram_tensor("x", [T_TOK, D], F32, kind="ExternalInput").ap()
    wq_h = nc.dram_tensor("Wq", [D, D], F32, kind="ExternalInput").ap()
    wk_h = nc.dram_tensor("Wk", [D, D], F32, kind="ExternalInput").ap()
    wv_h = nc.dram_tensor("Wv", [D, D], F32, kind="ExternalInput").ap()
    wo_h = nc.dram_tensor("Wo", [D, D], F32, kind="ExternalInput").ap()
    wf_h = nc.dram_tensor("Wf", [D, D], F32, kind="ExternalInput").ap()
    wg1_h = nc.dram_tensor("Wg1", [128, D], F32, kind="ExternalInput").ap()
    wg2_h = nc.dram_tensor("Wg2", [D, 128], F32, kind="ExternalInput").ap()
    g_h = nc.dram_tensor("gnw", [1, D], F32, kind="ExternalInput").ap()
    b_h = nc.dram_tensor("fbias", [1, D], F32, kind="ExternalInput").ap()
    idn_h = nc.dram_tensor("c_idn", [128, 128], BF16, kind="ExternalInput").ap()
    tri_i_h = nc.dram_tensor("c_tri_i", [128, 128], BF16, kind="ExternalInput").ap()
    tri_s_h = nc.dram_tensor("c_tri_s", [128, 128], BF16, kind="ExternalInput").ap()
    mask_h = nc.dram_tensor("c_mask", [128, 128], F32, kind="ExternalInput").ap()
    ones_h = nc.dram_tensor("c_ones", [1, 128], BF16, kind="ExternalInput").ap()
    f32s_h = nc.dram_tensor("c_f32s", [128, 8], F32, kind="ExternalInput").ap()
    out_h = nc.dram_tensor("out", [T_TOK, D], F32, kind="ExternalOutput").ap()

    sc_q = nc.dram_tensor("sc_q", [D, D], BF16, kind="Internal").ap()
    sc_k = nc.dram_tensor("sc_k", [D, D], BF16, kind="Internal").ap()
    sc_v = nc.dram_tensor("sc_v", [D, D], BF16, kind="Internal").ap()
    sc_o = nc.dram_tensor("sc_o", [D, D], BF16, kind="Internal").ap()
    sc_g1 = nc.dram_tensor("sc_g1", [128, D], BF16, kind="Internal").ap()
    sc_g2 = nc.dram_tensor("sc_g2", [D, 128], BF16, kind="Internal").ap()
    sc_wd = nc.dram_tensor("sc_wd", [D, D], BF16, kind="Internal").ap()

    dbg = {}
    if debug:
        for name in ["d_etok", "d_r", "d_qtok", "d_vtok", "d_o2", "d_x2", "d_xn3",
                     "d_xd", "d_wd", "d_xn1", "d_wk0", "d_xn1f",
                     "d_abf", "d_wm", "d_opre", "d_ef", "d_qf",
                     "d_qi", "d_qd", "d_qc", "d_qm", "d_qmv", "d_qu", "d_qfl",
                     "d_fps", "d_x3", "d_xdf"]:
            dbg[name] = nc.dram_tensor(name, [128, D], F32, kind="ExternalOutput").ap()

    es = ExitStack()
    S = Sched()

    def sb(name, shape, dt):
        return es.enter_context(nc.sbuf_tensor(name, shape, dt))

    def psum(name, shape, dt):
        return es.enter_context(nc.psum_tensor(name, shape, dt))

    with es:
        wkT = [sb(f"wkT{i}", [128, D], BF16) for i in range(8)]
        wqT = [sb(f"wqT{i}", [128, D], BF16) for i in range(8)]
        wvT = [sb(f"wvT{i}", [128, D], BF16) for i in range(8)]
        woT = [sb(f"woT{i}", [128, D], BF16) for i in range(8)]
        wdT = [sb(f"wdT{i}", [128, D], BF16) for i in range(8)]
        wg1T = sb("wg1T", [128, 8 * 128], BF16)
        wg2T = sb("wg2T", [128, D], BF16)
        g_rep = sb("g_rep", [128, D], F32)
        grow = sb("grow", [1, D], F32)
        ghi = sb("ghi", [1, D], BF16)
        glo = sb("glo", [1, D], BF16)
        brow = sb("brow", [1, D], F32)
        bhi = sb("bhi", [1, D], BF16)
        blo = sb("blo", [1, D], BF16)
        idn = sb("idn", [128, 128], BF16)
        tri_i = sb("tri_i", [128, 128], BF16)
        tri_s = sb("tri_s", [128, 128], BF16)
        cmask = sb("cmask", [128, 128], F32)
        ones_r = sb("ones_r", [1, 128], BF16)
        f32s = sb("f32s", [128, 8], F32)
        for _ci, _cv in enumerate(CVALS):
            nc.const_aps.register(F32, float(_cv), f32s[:, _ci:_ci + 1])

        xin = [sb(f"xin{i}", [128, D], F32) for i in range(2)]
        x2b = [sb(f"x2b{i}", [128, D], F32) for i in range(3)]
        x3b = [sb(f"x3b{i}", [128, D], F32) for i in range(2)]

        xn1tok = sb("xn1tok", [128, D], BF16)
        xn1f = sb("xn1f", [128, D], BF16)
        etok = sb("etok", [128, D], BF16)
        efeat = sb("efeat", [128, D], BF16)
        qtok = sb("qtok", [128, D], BF16)
        qfeat = sb("qfeat", [128, D], BF16)
        vtok = sb("vtok", [128, D], BF16)
        g1f = sb("g1f", [128, 128], BF16)
        r_sb = sb("r_sb", [128, D], F32)
        sg = sb("sg", [128, D], F32)
        wmask = sb("wmask", [128, D], BF16)
        a_bf = sb("a_bf", [128, D], BF16)
        a_sb = sb("a_sb", [128, D], F32)
        o2t = sb("o2t", [128, D], BF16)
        o2f = sb("o2f", [128, D], BF16)
        xn3 = sb("xn3", [128, D], F32)
        xd = sb("xd", [128, D], BF16)
        xdf = sb("xdf", [128, D], BF16)

        sF = [sb(f"sF{i}", [128, D], F32) for i in range(3)]
        sB = [sb(f"sB{i}", [128, D], BF16) for i in range(3)]
        sB.append(sb("sB3", [128, D], F32))
        qF = [sb(f"qF{i}", [128, D], F32) for i in range(2)]
        stt = [sb(f"st{i}", [128, 1], F32) for i in range(7)]
        stf = [sb(f"stf{i}", [128, 1], F32) for i in range(5)]
        st1 = [sb(f"st1_{i}", [128, 1], F32) for i in range(5)]
        bm32 = sb("bm32", [128, 32], F32)
        s32 = sb("s32", [128, 32], F32)

        c_run = psum("c_run", [128, D], F32)
        P = [psum("P0", [128, D], F32), psum("P1", [128, D], F32)]
        tpB = psum("tpB", [128, D], BF16)
        tpB2 = psum("tpB2", [128, D], BF16)
        tp_rot = [0]
        tp_banks = [tpB, tpB2]

        sems = {}
        for name in ["ld", "l2", "l3", "l4", "l5", "gd", "g2", "g3", "g4", "g5",
                     "st", "t", "v", "a", "p"]:
            sems[name] = es.enter_context(nc.semaphore(name))

        def dma(out, in_):
            return lambda e: e.dma_start(out=out, in_=in_)

        t_consts = None
        for dst, src in [(idn[:], idn_h), (tri_i[:], tri_i_h), (tri_s[:], tri_s_h),
                         (cmask[:], mask_h), (ones_r[:], ones_h),
                         (f32s[:], f32s_h),
                         (grow[:], g_h), (brow[:], b_h)]:
            t_consts = S.add("sync", dma(dst, src), deps=[t_consts], inc="ld",
                             inc_n=16, dur=0.5)
        S.const_tok = t_consts

        x_loaded = [None] * NT
        for _tau in range(min(2, NT)):
            x_loaded[_tau] = S.add("sync",
                                   dma(xin[_tau % 2][:],
                                       x_h[_tau * 128:(_tau + 1) * 128, :]),
                                   inc="ld", inc_n=16, dur=3.0)

        cast_toks = {}
        for nm, dst, src, sem in [("k", sc_k, wk_h, "gd"), ("q", sc_q, wq_h, "g2"),
                                  ("v", sc_v, wv_h, "g3"), ("o", sc_o, wo_h, "g4"),
                                  ("g1", sc_g1, wg1_h, "gd"),
                                  ("g2", sc_g2, wg2_h, "g2")]:
            cast_toks[nm] = S.add("gpsimd", dma(dst, src), inc=sem, inc_n=16,
                                  dur=(30.0 if nm in ("k", "q", "v", "o") else 2.0))

        wT_tok = {}
        tsem = {"k": "l2", "q": "l3", "v": "l4", "o": "l5"}
        tsrc = {"k": sc_k, "q": sc_q, "v": sc_v, "o": sc_o}
        tdst = {"k": wkT, "q": wqT, "v": wvT, "o": woT}
        for i in range(8):
            for nm in ["k", "q", "v", "o"]:
                wT_tok[nm] = S.add(
                    "sync",
                    (lambda dst, s: lambda e: e.dma_start_transpose(out=dst, in_=s))(
                        tdst[nm][i][:], tsrc[nm][0:D, i * 128:(i + 1) * 128]),
                    deps=[cast_toks[nm], wT_tok.get(nm)], inc=tsem[nm], inc_n=16,
                    dur=4.0)
        tk = None
        for i in range(8):
            tk = S.add("sync",
                       (lambda i: lambda e: e.dma_start_transpose(
                           out=wg1T[:, i * 128:(i + 1) * 128],
                           in_=sc_g1[0:128, i * 128:(i + 1) * 128]))(i),
                       deps=[cast_toks["g1"], tk], inc="l3", inc_n=16, dur=1.0)
        wT_tok["g1"] = tk
        wT_tok["g2"] = S.add("sync",
                             lambda e: e.dma_start_transpose(out=wg2T[:],
                                                             in_=sc_g2[0:D, 0:128]),
                             deps=[cast_toks["g2"]], inc="l2", inc_n=16, dur=1.0)

        tg1 = S.add("vector", lambda e: e.tensor_copy(out=ghi[:], in_=grow[:]),
                    deps=[t_consts], inc="v", dur=0.2)
        tg2 = S.add("vector", lambda e: e.tensor_tensor(out=glo[:], in0=grow[:],
                                                        in1=ghi[:], op=ALU.subtract),
                    deps=[tg1], inc="v", dur=0.2)
        tb1 = S.add("vector", lambda e: e.tensor_copy(out=bhi[:], in_=brow[:]),
                    deps=[t_consts], inc="v", dur=0.2)
        tb2 = S.add("vector", lambda e: e.tensor_tensor(out=blo[:], in0=brow[:],
                                                        in1=bhi[:], op=ALU.subtract),
                    deps=[tb1], inc="v", dur=0.2)

        def g_mms(e):
            mm = None
            for half in range(2):
                sl = slice(half * 512, (half + 1) * 512)
                mm = e.matmul(out=P[0][:, sl], lhsT=ones_r[:], rhs=ghi[:, sl],
                              start=True, stop=False)
                mm = e.matmul(out=P[0][:, sl], lhsT=ones_r[:], rhs=glo[:, sl],
                              start=False, stop=True)
            return mm

        tgm = S.add("tensor", g_mms, deps=[tg2, t_consts], inc="t")
        tgrep = S.add("scalar", lambda e: e.activation(out=g_rep[:], in_=P[0][:],
                                                       func=AF.Identity),
                      deps=[tgm], inc="a")

        pair_free = [tgrep, Tok("t", 0)]

        def quantize(src_f32, src_tok, dst_bf16, dst_handle, taps=False):
            def tap(name, ap_fn, dep, handle=None):
                if not taps:
                    return
                gtk = S.add("gpsimd", (lambda a=ap_fn, nm=name: (lambda e: e.dma_start(
                    out=dbg["d_q" + nm], in_=a)))(), deps=[dep], inc="gd", inc_n=16)
                if handle is not None:
                    S.buse(handle, gtk)
                return gtk
            """MX-FP6 quantize [128,1024] f32 -> dequantized bf16 in dst.

            Region selectors are captured as int8 tensors so the fp64 shadow
            (which reads int tensors at native values) uses the SAME grid
            region per element as the native pass; the robust floor
            (i - 1[i > z]) then yields the exact fp64 rounding."""
            u, zt, fl = qF[0], qF[1], sF[2]
            au, mfac, minv, corr = sB[0], sB[1], sB[2], sB[3]
            t0 = S.add("vector",
                       lambda e: e.reduce_max(out=bm32[:], in_=src_f32.rearrange(
                           "p (g k) -> p g k", k=32), axis=AX.X,
                           apply_absolute_value=True),
                       deps=[src_tok, S.bfree(bm32)], inc="v")
            S.buse(bm32, t0)
            t1 = S.add("vector",
                       lambda e: e.tensor_scalar(out=s32[:], in0=bm32[:],
                                                 scalar1=1.0 / 7.5, scalar2=1e-30,
                                                 op0=ALU.mult, op1=ALU.max),
                       deps=[t0, S.bfree(s32)], inc="v", dur=0.2)
            t2 = S.add("vector", lambda e: e.reciprocal(out=bm32[:], in_=s32[:]),
                       deps=[t1], inc="v", dur=0.35)
            src3 = src_f32.rearrange("p (g k) -> p g k", k=32)
            inv3 = bm32[:].rearrange("p (g o) -> p g o", o=1).broadcast_to([128, 32, 32])
            s3 = s32[:].rearrange("p (g o) -> p g o", o=1).broadcast_to([128, 32, 32])
            tu = S.add("vector",
                       lambda e: e.tensor_tensor(
                           out=u[:].rearrange("p (g k) -> p g k", k=32),
                           in0=src3, in1=inv3, op=ALU.mult),
                       deps=[t2, src_tok, S.bfree(u)], inc="v")
            S.buse(bm32, tu)
            tau_ = S.add("scalar", lambda e: e.activation(out=au[:], in_=u[:],
                                                          func=AF.Abs),
                         deps=[tu, S.bfree(au)], inc="a")
            ts1 = S.add("vector",
                        lambda e: e.tensor_scalar(out=corr[:], in0=au[:], scalar1=2.0,
                                                  scalar2=-4.0, op0=ALU.is_ge,
                                                  op1=ALU.mult),
                        deps=[tau_, S.bfree(corr)], inc="v", dur=0.6)
            ts2 = S.add("vector",
                        lambda e: e.tensor_scalar(out=minv[:], in0=au[:], scalar1=4.0,
                                                  scalar2=-2.0, op0=ALU.is_ge,
                                                  op1=ALU.mult),
                        deps=[tau_, S.bfree(minv)], inc="v", dur=0.6)
            tm4 = S.add("vector",
                        lambda e: e.scalar_tensor_tensor(out=mfac[:], in0=corr[:],
                                                         scalar=8.0, in1=minv[:],
                                                         op0=ALU.add, op1=ALU.add),
                        deps=[ts1, ts2, S.bfree(mfac)], inc="v")
            tv1 = S.add("vector",
                        lambda e: e.tensor_scalar(out=minv[:], in0=au[:], scalar1=2.0,
                                                  scalar2=0.125, op0=ALU.is_ge,
                                                  op1=ALU.mult),
                        deps=[tau_, tm4], inc="v", dur=0.6)
            tv2 = S.add("vector",
                        lambda e: e.tensor_scalar(out=au[:], in0=au[:], scalar1=4.0,
                                                  scalar2=0.25, op0=ALU.is_ge,
                                                  op1=ALU.mult),
                        deps=[tau_, ts1, ts2, tv1], inc="v", dur=0.6)
            tmv3 = S.add("vector",
                         lambda e: e.scalar_tensor_tensor(out=minv[:], in0=minv[:],
                                                          scalar=0.125, in1=au[:],
                                                          op0=ALU.add, op1=ALU.add),
                         deps=[tv1, tv2], inc="v")
            S.buse(au, tmv3)
            S.buse(corr, tm4)
            ty = S.add("vector",
                       lambda e: e.tensor_tensor(out=zt[:], in0=u[:], in1=mfac[:],
                                                 op=ALU.mult),
                       deps=[tm4, tu, S.bfree(zt)], inc="v")
            S.buse(mfac, ty)
            S.buse(u, ty)
            tw0 = S.add("scalar",
                        lambda e: e.activation(out=u[:], in_=zt[:], func=AF.Identity,
                                               scale=-1.0, bias=16.5),
                        deps=[ty], inc="a")
            prevt = tw0
            prev_tile = u
            seq = [zt, fl, zt, fl, zt, fl]
            used_fl = False
            for bi, bval in enumerate([32.0, 16.0, 8.0, 4.0, 2.0, 1.0]):
                dst = seq[bi]
                extra = []
                if dst is fl and not used_fl:
                    extra = S.bfree(fl)
                    used_fl = True
                tcmp = S.add("vector",
                             (lambda w=prev_tile, b=bval: lambda e: e.tensor_scalar(
                                 out=corr[:], in0=w[:], scalar1=b, scalar2=-b,
                                 op0=ALU.is_ge, op1=ALU.mult))(),
                             deps=[prevt] + (S.bfree(corr) if bi == 0 else []),
                             inc="v", dur=0.6)
                steng = "gpsimd" if bi % 2 == 1 else "vector"
                tstep = S.add(steng,
                              (lambda w=prev_tile, d=dst: lambda e: e.tensor_tensor(
                                  out=d[:], in0=w[:], in1=corr[:], op=ALU.add))(),
                              deps=[tcmp, extra], inc=ENG_SEM[steng],
                              dur=(3.0 if steng == "gpsimd" else 1.25))
                prevt = tstep
                prev_tile = dst
            S.buse(corr, prevt)
            tf2 = S.add("vector",
                        lambda e, wf=prev_tile: e.scalar_tensor_tensor(
                            out=zt[:], in0=wf[:], scalar=16.0, in1=u[:],
                            op0=ALU.add, op1=ALU.subtract),
                        deps=[prevt], inc="v")
            S.buse(u, tf2)
            S.buse(fl, tf2)
            if taps:
                tap("fl", zt[:], tf2, handle=zt)
                tap("mv", minv[:], tmv3, handle=minv)
            tw = S.add("vector",
                       lambda e: e.tensor_tensor(out=u[:], in0=zt[:], in1=minv[:],
                                                 op=ALU.mult),
                       deps=[tf2, tmv3], inc="v")
            S.buse(zt, tw)
            S.buse(minv, tw)
            txd = S.add("vector",
                        lambda e: e.tensor_tensor(
                            out=dst_bf16.rearrange("p (g k) -> p g k", k=32),
                            in0=u[:].rearrange("p (g k) -> p g k", k=32),
                            in1=s3, op=ALU.mult),
                        deps=[tw, t1, S.bfree(dst_handle)], inc="v")
            S.buse(u, txd)
            S.buse(s32, txd)
            return txd

        wq_state = {"tok": None}

        def wquant_stages():
            wd_done = []
            for j in range(8):
                tld = S.add("sync", dma(x2b[2][:], wf_h[j * 128:(j + 1) * 128, :]),
                            deps=[S.bfree(x2b[2])], inc="l5", inc_n=16, dur=3.0)
                tq = quantize(x2b[2][:], tld, xd[:], xd)
                S.buse(x2b[2], tq)
                tst = S.add("gpsimd", dma(sc_wd[j * 128:(j + 1) * 128, :], xd[:]),
                            deps=[tq], inc="g5", inc_n=16, dur=1.0)
                S.buse(xd, tst)
                wd_done.append(tst)
                yield
            tk = None
            for i in range(8):
                tk = S.add("sync",
                           (lambda i: lambda e: e.dma_start_transpose(
                               out=wdT[i][:], in_=sc_wd[0:D, i * 128:(i + 1) * 128]))(i),
                           deps=wd_done + [tk], inc="l4", inc_n=16, dur=4.0)
            wq_state["tok"] = tk

        def transpose8(src_bf16, src_handle, dst_bf16, dst_handle, pair, deps_in):
            tp = tp_banks[tp_rot[0] % 2]
            use_act = (tp_rot[0] % 2 == 1)
            tp_rot[0] += 1

            def fn(e, src_bf16=src_bf16, tp=tp):
                mm = None
                for h in range(8):
                    mm = e.transpose(out=tp[:, h * 128:(h + 1) * 128],
                                     in_=src_bf16[:, h * 128:(h + 1) * 128],
                                     identity=idn[:])
                return mm
            tk = S.add("tensor", fn,
                       deps=list(deps_in) + [t_consts, S.bfree(tp)], inc="t", dur=0.6)
            S.buse(src_handle, tk)
            if use_act:
                ev = S.add("scalar",
                           lambda e, dst_bf16=dst_bf16, tp=tp: e.activation(
                               out=dst_bf16, in_=tp[:], func=AF.Identity),
                           deps=[tk, S.bfree(dst_handle)], inc="a", dur=1.2)
            else:
                ev = S.add("vector",
                           lambda e, dst_bf16=dst_bf16, tp=tp: e.tensor_copy(
                               out=dst_bf16, in_=tp[:]),
                           deps=[tk, S.bfree(dst_handle)], inc="v", dur=1.2)
            S.buse(tp, ev)
            return ev

        def proj16(xnf, wT, pair, deps_in):
            def fn(e, xnf=xnf, wT=wT, pair=pair):
                mm = None
                for half in range(2):
                    sl = slice(half * 512, (half + 1) * 512)
                    for i in range(8):
                        mm = e.matmul(out=P[pair][:, sl],
                                      lhsT=xnf[:, i * 128:(i + 1) * 128],
                                      rhs=wT[i][:, sl], start=(i == 0), stop=(i == 7))
                return mm
            return S.add("tensor", fn, deps=list(deps_in) + [pair_free[pair]],
                         inc="t", dur=5.0)

        def ln_apply(src_ap, src_tok, is_x2_add, out_ap, out_handle,
                     extra_dep=None, pool=None):
            sq, stats, lnt = pool
            mN, vN, rN, nN, mmN = stats
            skey = mN
            ta = S.add("scalar",
                       lambda e: e.activation(out=sq[:], in_=src_ap, func=AF.Square,
                                              accum_out=vN[:]),
                       deps=[src_tok, S.bfree(sq), S.bfree(skey)], inc="a", dur=1.3)
            tbv = S.add("scalar",
                        lambda e: e.activation(out=sq[:], in_=src_ap,
                                               func=AF.Identity, accum_out=mN[:]),
                        deps=[src_tok, ta], inc="a", dur=1.3)
            S.buse(sq, tbv)
            tcv = S.add("vector",
                        lambda e: e.tensor_scalar(out=mN[:], in0=mN[:], scalar1=1.0 / D,
                                                  scalar2=None, op0=ALU.mult),
                        deps=[tbv], inc="v", dur=0.15)
            tdv = S.add("vector",
                        lambda e: e.tensor_scalar(out=vN[:], in0=vN[:], scalar1=1.0 / D,
                                                  scalar2=None, op0=ALU.mult),
                        deps=[ta], inc="v", dur=0.15)
            tev = S.add("vector",
                        lambda e: e.tensor_tensor(out=mmN[:], in0=mN[:], in1=mN[:],
                                                  op=ALU.mult),
                        deps=[tcv], inc="v", dur=0.15)
            tfv = S.add("vector",
                        lambda e: e.tensor_tensor(out=vN[:], in0=vN[:], in1=mmN[:],
                                                  op=ALU.subtract),
                        deps=[tdv, tev], inc="v", dur=0.15)
            tgv = S.add("scalar",
                        lambda e: e.activation(out=rN[:], in_=vN[:], func=AF.Ln,
                                               bias=EPS),
                        deps=[tfv], inc="a", dur=0.3)
            thv = S.add("scalar",
                        lambda e: e.activation(out=rN[:], in_=rN[:], func=AF.Exp,
                                               scale=-0.5),
                        deps=[tgv], inc="a", dur=0.3)
            tiv = S.add("vector",
                        lambda e: e.scalar_tensor_tensor(out=nN[:], in0=mN[:],
                                                         scalar=-1.0, in1=rN[:],
                                                         op0=ALU.mult, op1=ALU.mult),
                        deps=[thv, tcv], inc="v", dur=0.15)
            if is_x2_add is None:
                tfin = S.add("scalar",
                             lambda e: e.activation(out=out_ap, in_=src_ap,
                                                    func=AF.Identity, scale=rN[:],
                                                    bias=nN[:]),
                             deps=[thv, tiv, S.bfree(out_handle), extra_dep],
                             inc="a", dur=1.3)
            else:
                x_old_ap, x_old_tok = is_x2_add
                tmid = S.add("scalar",
                             lambda e, lnt=lnt: e.activation(out=lnt[:], in_=src_ap,
                                                             func=AF.Identity,
                                                             scale=rN[:], bias=nN[:]),
                             deps=[thv, tiv, S.bfree(lnt), extra_dep], inc="a",
                             dur=1.3)
                tfin = S.add("gpsimd",
                             lambda e, lnt=lnt: e.tensor_tensor(out=out_ap, in0=lnt[:],
                                                                in1=x_old_ap,
                                                                op=ALU.add),
                             deps=[tmid, x_old_tok, S.bfree(out_handle)], inc="p",
                             dur=3.0)
                S.buse(lnt, tfin)
            S.buse(skey, tfin)
            return tfin

        c_free = Tok("v", 0)
        aps_written = Tok("t", 0)
        out_done = []
        x2tok = {}

        def ffn_stages(ft):
            X2 = x2b[ft % 3]
            X3 = x3b[ft % 2]
            fp = 1 - (ft % 2)
            t_x2 = x2tok[ft]
            ln3_pool = (qF[0], (stf[0], stf[1], stf[2], stf[3], stf[4]), None)
            t_xn3 = ln_apply(X2[:], t_x2, None, xn3[:], xn3, pool=ln3_pool)
            yield
            t_xd = quantize(xn3[:], t_xn3, xd[:], xd)
            S.buse(xn3, t_xd)
            yield
            t_xdf = transpose8(xd[:], xd, xdf[:], xdf, fp, [t_xd])
            tF = proj16(xdf[:], wdT, fp, [t_xdf, wq_state["tok"]])
            S.buse(xdf, tF)

            def bias_fn(e, fp=fp):
                mm = None
                for half in range(2):
                    sl = slice(half * 512, (half + 1) * 512)
                    mm = e.matmul(out=P[fp][:, sl], lhsT=ones_r[:], rhs=bhi[:, sl],
                                  start=False, stop=False, skip_group_check=True)
                    mm = e.matmul(out=P[fp][:, sl], lhsT=ones_r[:], rhs=blo[:, sl],
                                  start=False, stop=True, skip_group_check=True)
                return mm
            tFb = S.add("tensor", bias_fn, deps=[tF, tb2, t_consts], inc="t")
            ln4_pool = (qF[0], (stf[0], stf[1], stf[2], stf[3], stf[4]), qF[1])
            t_x3 = ln_apply(P[fp][:], tFb, (X2[:], t_x2), X3[:], X3, pool=ln4_pool)
            pair_free[fp] = t_x3
            S.buse(X2, t_x3)
            t_out = S.add("gpsimd", dma(out_h[ft * 128:(ft + 1) * 128, :], X3[:]),
                          deps=[t_x3], inc="st", inc_n=16, dur=3.0)
            S.buse(X3, t_out)
            out_done.append(t_out)


        def attn_stages(tau):
            nonlocal c_free, aps_written
            X = xin[tau % 2]
            X2 = x2b[tau % 3]
            pr = tau % 2

            ln1_pool = (sF[0], (st1[0], st1[1], st1[2], st1[3], st1[4]), None)
            t_xn1 = ln_apply(X[:], x_loaded[tau], None, xn1tok[:], xn1tok,
                             pool=ln1_pool)
            t_xn1f = transpose8(xn1tok[:], xn1tok, xn1f[:], xn1f, pr, [t_xn1])

            yield
            tK = proj16(xn1f[:], wkT, pr, [t_xn1f, wT_tok["k"], S.bfree(etok)])
            t_e = S.add("scalar",
                        lambda e, pp=P[pr]: e.activation(out=etok[:], in_=pp[:],
                                                         func=AF.Exp),
                        deps=[tK], inc="a")
            pair_free[pr] = t_e

            def c_incl(e, tau=tau):
                mm = None
                for half in range(2):
                    sl = slice(half * 512, (half + 1) * 512)
                    mm = e.matmul(out=c_run[:, sl], lhsT=tri_i[:], rhs=etok[:, sl],
                                  start=(tau == 0), stop=False,
                                  skip_group_check=True)
                return mm
            tci = S.add("tensor", c_incl, deps=[t_e, c_free, t_consts], inc="t")
            t_r0 = S.add("scalar",
                         lambda e: e.activation(out=r_sb[:], in_=c_run[:],
                                                func=AF.Ln),
                         deps=[tci, S.bfree(r_sb)], inc="a")
            t_r = S.add("scalar",
                        lambda e: e.activation(out=r_sb[:], in_=r_sb[:],
                                               func=AF.Exp, scale=-1.0),
                        deps=[t_r0], inc="a")
            c_free = t_r0


            yield
            tQ = proj16(xn1f[:], wqT, 1 - pr, [t_xn1f, wT_tok["q"]])
            silq = sF[1]
            t_sq = S.add("scalar",
                         lambda e, pp=P[1 - pr]: e.activation(out=silq[:], in_=pp[:],
                                                              func=AF.Silu),
                         deps=[tQ, S.bfree(silq)], inc="a", tag="silu")
            pair_free[1 - pr] = t_sq
            t_qt = S.add("gpsimd",
                         lambda e: e.tensor_tensor(out=qtok[:], in0=silq[:],
                                                   in1=r_sb[:], op=ALU.mult),
                         deps=[t_sq, t_r, S.bfree(qtok)], inc="p", dur=3.0)
            S.buse(silq, t_qt)
            S.buse(r_sb, t_qt)
            def c_strict(e):
                mm = None
                for half in range(2):
                    sl = slice(half * 512, (half + 1) * 512)
                    mm = e.matmul(out=c_run[:, sl], lhsT=tri_s[:], rhs=etok[:, sl],
                                  start=False, stop=True, skip_group_check=True)
                return mm
            tcs = S.add("tensor", c_strict, deps=[t_r], inc="t")

            yield
            tV = proj16(xn1f[:], wvT, pr, [t_xn1f, wT_tok["v"], t_e, S.bfree(vtok)])
            t_v = S.add("scalar",
                        lambda e, pp=P[pr]: e.activation(out=vtok[:], in_=pp[:],
                                                         func=AF.Identity),
                        deps=[tV], inc="a")
            pair_free[pr] = t_v

            def g1_fn(e, pr=pr):
                mm = None
                for i in range(8):
                    mm = e.matmul(out=P[1 - pr][:, 0:128],
                                  lhsT=wg1T[:, i * 128:(i + 1) * 128],
                                  rhs=xn1f[:, i * 128:(i + 1) * 128],
                                  start=(i == 0), stop=(i == 7))
                return mm
            tG1 = S.add("tensor", g1_fn,
                        deps=[t_xn1f, wT_tok["g1"], pair_free[1 - pr]], inc="t")
            t_g1 = S.add("vector",
                         lambda e, pp=P[1 - pr]: e.tensor_copy(out=g1f[:],
                                                               in_=pp[:, 0:128]),
                         deps=[tG1, S.bfree(g1f)], inc="v")
            pair_free[1 - pr] = t_g1

            def g2_fn(e, pr=pr):
                mm = None
                for half in range(2):
                    sl = slice(half * 512, (half + 1) * 512)
                    mm = e.matmul(out=P[1 - pr][:, sl], lhsT=g1f[:], rhs=wg2T[:, sl],
                                  start=True, stop=True)
                return mm
            tG2 = S.add("tensor", g2_fn, deps=[t_g1, wT_tok["g2"], pair_free[1 - pr]],
                        inc="t")
            S.buse(g1f, tG2)
            t_sg = S.add("scalar",
                         lambda e, pp=P[1 - pr]: e.activation(out=sg[:], in_=pp[:],
                                                              func=AF.Silu),
                         deps=[tG2, S.bfree(sg)], inc="a", tag="silu")
            pair_free[1 - pr] = t_sg

            yield
            t_ef = transpose8(etok[:], None, efeat[:], efeat, pr, [t_e, tcs])
            t_qf = transpose8(qtok[:], qtok, qfeat[:], qfeat, 1 - pr, [t_qt])

            if tau > 0:
                t_abf = S.add("scalar",
                              lambda e: e.activation(out=a_bf[:], in_=a_sb[:],
                                                     func=AF.Identity, scale=SCALE),
                              deps=[aps_written, S.bfree(a_bf)], inc="a")
            else:
                t_abf = None

            def a_upd(e, pr=pr):
                mm = None
                for h in range(8):
                    sl = slice(h * 128, (h + 1) * 128)
                    mm = e.matmul(out=P[pr][:, sl], lhsT=etok[:, sl], rhs=vtok[:, sl],
                                  start=(h % 4 == 0), stop=(h % 4 == 3))
                return mm
            tAu = S.add("tensor", a_upd, deps=[t_e, t_v, pair_free[pr]], inc="t")
            S.buse(etok, tAu)
            if tau > 0:
                tAa = S.add("vector",
                            lambda e, pp=P[pr]: e.tensor_tensor(
                                out=a_sb[:], in0=pp[:], in1=a_sb[:], op=ALU.add),
                            deps=[tAu, t_abf], inc="v")
            else:
                tAa = S.add("vector",
                            lambda e, pp=P[pr]: e.tensor_copy(out=a_sb[:], in_=pp[:]),
                            deps=[tAu], inc="v")
            pair_free[pr] = tAa
            aps_written = tAa

            yield
            def wr_fn(e, pr=pr):
                mm = None
                for h in range(8):
                    sl = slice(h * 128, (h + 1) * 128)
                    mm = e.matmul(out=P[pr][:, sl], lhsT=efeat[:, sl],
                                  rhs=qfeat[:, sl], start=True, stop=True)
                return mm
            tW = S.add("tensor", wr_fn, deps=[t_ef, t_qf, pair_free[pr]], inc="t")
            S.buse(efeat, tW)
            mask3 = cmask[:].rearrange("p (o n) -> p o n", o=1).broadcast_to(
                [128, 8, 128])
            t_wm = S.add("vector",
                         lambda e, pp=P[pr]: e.tensor_tensor(
                             out=wmask[:].rearrange("p (h n) -> p h n", n=128),
                             in0=pp[:].rearrange("p (h n) -> p h n", n=128),
                             in1=mask3, op=ALU.mult),
                         deps=[tW, S.bfree(wmask)], inc="v")
            pair_free[pr] = t_wm

            def o_fn(e, pr=pr, first=(tau == 0)):
                mm = None
                for h in range(8):
                    sl = slice(h * 128, (h + 1) * 128)
                    mm = e.matmul(out=P[1 - pr][:, sl], lhsT=wmask[:, sl],
                                  rhs=vtok[:, sl], start=True, stop=first)
                    if not first:
                        mm = e.matmul(out=P[1 - pr][:, sl], lhsT=qfeat[:, sl],
                                      rhs=a_bf[:, sl], start=False, stop=True)
                return mm
            tO = S.add("tensor", o_fn,
                       deps=[t_wm, t_abf, t_qf, t_v, pair_free[1 - pr]], inc="t",
                       dur=2.9)
            S.buse(wmask, tO)
            S.buse(qfeat, tO)
            S.buse(vtok, tO)
            S.buse(a_bf, tO)

            o_scr = sF[0]
            v2, r2 = stt[5], stt[0]
            t_osq = S.add("scalar",
                          lambda e, pp=P[1 - pr]: e.activation(out=o_scr[:], in_=pp[:],
                                                               func=AF.Square,
                                                               accum_out=v2[:]),
                          deps=[tO, S.bfree(o_scr), S.bfree(v2)], inc="a")
            S.buse(o_scr, t_osq)
            t_or1 = S.add("scalar",
                          lambda e: e.activation(out=r2[:], in_=v2[:], func=AF.Ln,
                                                 scale=1.0 / D, bias=EPS),
                          deps=[t_osq], inc="a", dur=0.3)
            t_or2 = S.add("scalar",
                          lambda e: e.activation(out=r2[:], in_=r2[:], func=AF.Exp,
                                                 scale=-0.5),
                          deps=[t_or1], inc="a", dur=0.3)
            o_n2 = sF[0]
            t_on = S.add("vector",
                         lambda e, pp=P[1 - pr]: e.scalar_tensor_tensor(
                             out=o_n2[:], in0=pp[:], scalar=r2[:], in1=g_rep[:],
                             op0=ALU.mult, op1=ALU.mult),
                         deps=[t_or2, tgrep, S.bfree(o_n2)], inc="v")
            S.buse(v2, t_on)
            pair_free[1 - pr] = t_on
            t_o2 = S.add("gpsimd",
                         lambda e: e.tensor_tensor(out=o2t[:], in0=o_n2[:], in1=sg[:],
                                                   op=ALU.mult),
                         deps=[t_on, t_sg, S.bfree(o2t)], inc="p")
            S.buse(o_n2, t_o2)
            S.buse(sg, t_o2)

            yield
            t_o2f = transpose8(o2t[:], o2t, o2f[:], o2f, pr, [t_o2])
            tAO = proj16(o2f[:], woT, pr, [t_o2f, wT_tok["o"]])
            S.buse(o2f, tAO)
            ln2_pool = (sF[0], (stt[1], stt[2], stt[3], stt[4], stt[6]), sF[1])
            t_x2 = ln_apply(P[pr][:], tAO, (X[:], x_loaded[tau]), X2[:], X2,
                            pool=ln2_pool)
            pair_free[pr] = t_x2
            S.buse(X, t_x2)
            x2tok[tau] = t_x2

            if tau + 2 < NT:
                x_loaded[tau + 2] = S.add("sync",
                                          dma(xin[(tau + 2) % 2][:],
                                              x_h[(tau + 2) * 128:(tau + 3) * 128, :]),
                                          deps=[t_x2], inc="ld", inc_n=16, dur=3.0)

        wq = wquant_stages()
        for tau in range(NT):
            gens = [attn_stages(tau)]
            if tau == 0:
                gens.append(wq)
            elif tau == 1:
                gens.append(wq)
            if tau >= 2:
                gens.append(ffn_stages(tau - 2))
            while gens:
                for g in list(gens):
                    try:
                        next(g)
                    except StopIteration:
                        gens.remove(g)
        for _ in ffn_stages(NT - 2):
            pass
        for _ in ffn_stages(NT - 1):
            pass

        S.add("gpsimd", lambda e: e.engine_nop(), deps=out_done)

        S.reorder()

        with nc.Block() as block:
            def emit_for(engname):
                lastw = {}

                def run(eng):
                    for ent in S.entries:
                        ename, fn, deps, incs = ent[0], ent[1], ent[2], ent[3]
                        if ename != engname:
                            continue
                        for d in deps:
                            if lastw.get(d.sem, 0) < d.cnt:
                                eng.wait_ge(sems[d.sem], d.cnt)
                                lastw[d.sem] = d.cnt
                        for semname, n, base in incs:
                            if semname in DMA_SEMS and base > 0:
                                if lastw.get(semname, 0) < base:
                                    eng.wait_ge(sems[semname], base)
                                    lastw[semname] = base
                        instr = fn(eng)
                        for semname, n, base in incs:
                            instr.then_inc(sems[semname], n)
                return run

            block.sync(emit_for("sync"))
            block.gpsimd(emit_for("gpsimd"))
            block.vector(emit_for("vector"))
            block.scalar(emit_for("scalar"))
            block.tensor(emit_for("tensor"))

    return nc


def _consts():
    s = np.arange(128)[:, None]
    t = np.arange(128)[None, :]
    tri_i = (s <= t).astype(ml_dtypes.bfloat16)
    tri_s = (s > t).astype(ml_dtypes.bfloat16)
    mask = ((s <= t) * SCALE).astype(np.float32)
    idn = np.eye(128, dtype=ml_dtypes.bfloat16)
    ones = np.ones((1, 128), dtype=ml_dtypes.bfloat16)
    cvals = np.array(CVALS, dtype=np.float32)
    f32s = np.tile(cvals[None, :], (128, 1)).astype(np.float32)
    return idn, tri_i, tri_s, mask, ones, f32s


_NC_CACHE = {}


def kernel(L0_attn_hidden_states, L0_attn_Wq, L0_attn_Wk, L0_attn_Wv, L0_attn_Wg1,
           L0_attn_Wg2, L0_attn_g_norm_weight, L0_attn_Wo, L0_ffn_weight, L0_ffn_bias,
           _debug=False):
    x = np.ascontiguousarray(np.asarray(L0_attn_hidden_states, dtype=np.float32))
    idn, tri_i, tri_s, mask, ones, f32s = _consts()
    common = {
        "Wq": np.ascontiguousarray(np.asarray(L0_attn_Wq, np.float32)),
        "Wk": np.ascontiguousarray(np.asarray(L0_attn_Wk, np.float32)),
        "Wv": np.ascontiguousarray(np.asarray(L0_attn_Wv, np.float32)),
        "Wo": np.ascontiguousarray(np.asarray(L0_attn_Wo, np.float32)),
        "Wf": np.ascontiguousarray(np.asarray(L0_ffn_weight, np.float32)),
        "Wg1": np.ascontiguousarray(np.asarray(L0_attn_Wg1, np.float32)),
        "Wg2": np.ascontiguousarray(np.asarray(L0_attn_Wg2, np.float32)),
        "gnw": np.ascontiguousarray(np.asarray(L0_attn_g_norm_weight,
                                               np.float32).reshape(1, D)),
        "fbias": np.ascontiguousarray(np.asarray(L0_ffn_bias,
                                                 np.float32).reshape(1, D)),
        "c_idn": idn, "c_tri_i": tri_i, "c_tri_s": tri_s, "c_mask": mask,
        "c_ones": ones, "c_f32s": f32s,
    }
    import os
    dbg_tau = int(os.environ.get("KDBG_TAU", "0"))
    key = (bool(_debug), dbg_tau)
    if key not in _NC_CACHE:
        _NC_CACHE[key] = build_nc(debug=_debug, dbg_tau=dbg_tau)
    nc = _NC_CACHE[key]
    in_maps = []
    for b in range(N_CORES):
        m = dict(common)
        m["x"] = np.ascontiguousarray(x[b])
        in_maps.append(m)
    res = run_bass_kernel_spmd(nc, in_maps, core_ids=list(range(N_CORES)))
    out = np.stack([res.results[b]["out"] for b in range(N_CORES)], axis=0)
    if _debug:
        return out, res.results
    return out


# version 26
# speedup vs baseline: 1.0102x; 1.0102x over previous
import numpy as np
import ml_dtypes
from contextlib import ExitStack

import concourse.bass as bass
import concourse.mybir as mybir
from concourse.bass_utils import run_bass_kernel_spmd

F32 = mybir.dt.float32
BF16 = mybir.dt.bfloat16
I32 = mybir.dt.int32
AF = mybir.ActivationFunctionType
ALU = mybir.AluOpType
AX = mybir.AxisListType

T_TOK = 2048
D = 1024
NT = 16
SCALE = float(128) ** -0.5
EPS = 1e-5
N_CORES = 8
CVALS = [0.5, EPS, -0.5, -1.0, SCALE, 1.0 / D, 16.5, 0.0]

DMA_SEMS = {"ld", "l2", "l3", "l4", "l5", "gd", "g2", "g3", "g4", "g5", "st"}
ENG_SEM = {"tensor": "t", "vector": "v", "scalar": "a", "gpsimd": "p"}


class Tok:
    __slots__ = ("sem", "cnt")

    def __init__(self, sem, cnt):
        self.sem, self.cnt = sem, cnt


ENG_DUR = {"vector": 0.9, "scalar": 1.2, "tensor": 1.0, "gpsimd": 3.0, "sync": 3.0}


class Sched:
    def __init__(self):
        self.entries = []
        self.cnt = {}
        self.buf_free = {}
        self.const_tok = None

    def add(self, eng, fn, deps=(), inc=None, inc_n=1, dur=None, tag=None):
        flat = []
        for d in deps:
            if d is None:
                continue
            if isinstance(d, list):
                flat.extend(x for x in d if x is not None)
            else:
                flat.append(d)
        if eng == "scalar" and self.const_tok is not None:
            flat.append(self.const_tok)
        deps = flat
        incs = []
        tok = None
        if inc is not None:
            base = self.cnt.get(inc, 0)
            self.cnt[inc] = base + inc_n
            tok = Tok(inc, self.cnt[inc])
            incs.append((inc, inc_n, base))
        self.entries.append([eng, fn, deps, incs, tok, dur, tag])
        return tok

    def bfree(self, buf):
        return list(self.buf_free.get(id(buf), []))

    def buse(self, buf, tok):
        self.buf_free.setdefault(id(buf), []).append(tok)

    def reorder(self):
        ents = self.entries
        n = len(ents)
        prod = {}
        for i, ent in enumerate(ents):
            if ent[4] is not None:
                prod[id(ent[4])] = i
        preds = [set() for _ in range(n)]
        last_t = None
        for i, ent in enumerate(ents):
            for d in ent[2]:
                j = prod.get(id(d))
                if j is not None and j != i:
                    preds[i].add(j)
            if ent[0] == "tensor":
                if last_t is not None:
                    preds[i].add(last_t)
                last_t = i
        succs = [[] for _ in range(n)]
        for i in range(n):
            for j in preds[i]:
                succs[j].append(i)
        dur = [ent[5] if ent[5] is not None else ENG_DUR[ent[0]] for ent in ents]
        prio = [0.0] * n
        for i in range(n - 1, -1, -1):
            m = 0.0
            for s in succs[i]:
                if prio[s] > m:
                    m = prio[s]
            prio[i] = dur[i] + m
        indeg = [len(preds[i]) for i in range(n)]
        finish = [0.0] * n
        eng_free = {}
        sem_busy = {}
        ready = [i for i in range(n) if indeg[i] == 0]
        order = []
        cur_tab = "nat"
        while ready:
            best = None
            for i in ready:
                ent = ents[i]
                st = eng_free.get(ent[0], 0.0)
                for j in preds[i]:
                    if finish[j] > st:
                        st = finish[j]
                isdma = bool(ent[3]) and ent[3][0][0] in DMA_SEMS
                if isdma:
                    sb_ = sem_busy.get(ent[3][0][0], 0.0)
                    if sb_ > st:
                        st = sb_
                pen = 0.0
                if ent[0] == "scalar" and (ent[6] or "nat") != cur_tab:
                    pen = 1.3
                key = (st + pen, -prio[i], i)
                if best is None or key < best[0]:
                    best = (key, i, st + pen, isdma)
            _, i, st, isdma = best
            ready.remove(i)
            ent = ents[i]
            finish[i] = st + dur[i]
            if isdma:
                sem_busy[ent[3][0][0]] = finish[i]
                eng_free[ent[0]] = st + 0.05
            else:
                eng_free[ent[0]] = finish[i]
                if ent[0] == "scalar":
                    cur_tab = ent[6] or "nat"
            order.append(i)
            for s in succs[i]:
                indeg[s] -= 1
                if indeg[s] == 0:
                    ready.append(s)
        assert len(order) == n, "cycle in schedule graph"
        self.entries = [ents[i] for i in order]
        cnt = {}
        for ent in self.entries:
            newincs = []
            for (sem, nn, _b) in ent[3]:
                b = cnt.get(sem, 0)
                cnt[sem] = b + nn
                newincs.append((sem, nn, b))
                if ent[4] is not None and ent[4].sem == sem:
                    ent[4].cnt = b + nn
            ent[3][:] = newincs
        return max(finish) if n else 0.0


def build_nc(debug=False, dbg_tau=0):
    nc = bass.Bass("TRN2", target_bir_lowering=False)

    x_h = nc.dram_tensor("x", [T_TOK, D], F32, kind="ExternalInput").ap()
    wq_h = nc.dram_tensor("Wq", [D, D], F32, kind="ExternalInput").ap()
    wk_h = nc.dram_tensor("Wk", [D, D], F32, kind="ExternalInput").ap()
    wv_h = nc.dram_tensor("Wv", [D, D], F32, kind="ExternalInput").ap()
    wo_h = nc.dram_tensor("Wo", [D, D], F32, kind="ExternalInput").ap()
    wf_h = nc.dram_tensor("Wf", [D, D], F32, kind="ExternalInput").ap()
    wg1_h = nc.dram_tensor("Wg1", [128, D], F32, kind="ExternalInput").ap()
    wg2_h = nc.dram_tensor("Wg2", [D, 128], F32, kind="ExternalInput").ap()
    g_h = nc.dram_tensor("gnw", [1, D], F32, kind="ExternalInput").ap()
    b_h = nc.dram_tensor("fbias", [1, D], F32, kind="ExternalInput").ap()
    idn_h = nc.dram_tensor("c_idn", [128, 128], BF16, kind="ExternalInput").ap()
    tri_i_h = nc.dram_tensor("c_tri_i", [128, 128], BF16, kind="ExternalInput").ap()
    tri_s_h = nc.dram_tensor("c_tri_s", [128, 128], BF16, kind="ExternalInput").ap()
    mask_h = nc.dram_tensor("c_mask", [128, 128], F32, kind="ExternalInput").ap()
    ones_h = nc.dram_tensor("c_ones", [1, 128], BF16, kind="ExternalInput").ap()
    f32s_h = nc.dram_tensor("c_f32s", [128, 8], F32, kind="ExternalInput").ap()
    out_h = nc.dram_tensor("out", [T_TOK, D], F32, kind="ExternalOutput").ap()

    sc_q = nc.dram_tensor("sc_q", [D, D], BF16, kind="Internal").ap()
    sc_k = nc.dram_tensor("sc_k", [D, D], BF16, kind="Internal").ap()
    sc_v = nc.dram_tensor("sc_v", [D, D], BF16, kind="Internal").ap()
    sc_o = nc.dram_tensor("sc_o", [D, D], BF16, kind="Internal").ap()
    sc_g1 = nc.dram_tensor("sc_g1", [128, D], BF16, kind="Internal").ap()
    sc_g2 = nc.dram_tensor("sc_g2", [D, 128], BF16, kind="Internal").ap()
    sc_wd = nc.dram_tensor("sc_wd", [D, D], BF16, kind="Internal").ap()

    dbg = {}
    if debug:
        for name in ["d_etok", "d_r", "d_qtok", "d_vtok", "d_o2", "d_x2", "d_xn3",
                     "d_xd", "d_wd", "d_xn1", "d_wk0", "d_xn1f",
                     "d_abf", "d_wm", "d_opre", "d_ef", "d_qf",
                     "d_qi", "d_qd", "d_qc", "d_qm", "d_qmv", "d_qu", "d_qfl",
                     "d_fps", "d_x3", "d_xdf"]:
            dbg[name] = nc.dram_tensor(name, [128, D], F32, kind="ExternalOutput").ap()

    es = ExitStack()
    S = Sched()

    def sb(name, shape, dt):
        return es.enter_context(nc.sbuf_tensor(name, shape, dt))

    def psum(name, shape, dt):
        return es.enter_context(nc.psum_tensor(name, shape, dt))

    with es:
        wkT = [sb(f"wkT{i}", [128, D], BF16) for i in range(8)]
        wqT = [sb(f"wqT{i}", [128, D], BF16) for i in range(8)]
        wvT = [sb(f"wvT{i}", [128, D], BF16) for i in range(8)]
        woT = [sb(f"woT{i}", [128, D], BF16) for i in range(8)]
        wdT = [sb(f"wdT{i}", [128, D], BF16) for i in range(8)]
        wg1T = sb("wg1T", [128, 8 * 128], BF16)
        wg2T = sb("wg2T", [128, D], BF16)
        g_rep = sb("g_rep", [128, D], F32)
        grow = sb("grow", [1, D], F32)
        ghi = sb("ghi", [1, D], BF16)
        glo = sb("glo", [1, D], BF16)
        brow = sb("brow", [1, D], F32)
        bhi = sb("bhi", [1, D], BF16)
        blo = sb("blo", [1, D], BF16)
        idn = sb("idn", [128, 128], BF16)
        tri_i = sb("tri_i", [128, 128], BF16)
        tri_s = sb("tri_s", [128, 128], BF16)
        cmask = sb("cmask", [128, 128], F32)
        ones_r = sb("ones_r", [1, 128], BF16)
        f32s = sb("f32s", [128, 8], F32)
        for _ci, _cv in enumerate(CVALS):
            nc.const_aps.register(F32, float(_cv), f32s[:, _ci:_ci + 1])

        xin = [sb(f"xin{i}", [128, D], F32) for i in range(2)]
        x2b = [sb(f"x2b{i}", [128, D], F32) for i in range(3)]
        x3b = [sb(f"x3b{i}", [128, D], F32) for i in range(2)]

        xn1tok = sb("xn1tok", [128, D], BF16)
        xn1f = sb("xn1f", [128, D], BF16)
        etok = sb("etok", [128, D], BF16)
        efeat = sb("efeat", [128, D], BF16)
        qtok = sb("qtok", [128, D], BF16)
        qfeat = sb("qfeat", [128, D], BF16)
        vtok = sb("vtok", [128, D], BF16)
        g1f = sb("g1f", [128, 128], BF16)
        r_sb = sb("r_sb", [128, D], F32)
        sg = sb("sg", [128, D], F32)
        wmask = sb("wmask", [128, D], BF16)
        a_bf = sb("a_bf", [128, D], BF16)
        a_sb = sb("a_sb", [128, D], F32)
        o2t = sb("o2t", [128, D], BF16)
        o2f = sb("o2f", [128, D], BF16)
        xn3 = sb("xn3", [128, D], F32)
        xd = sb("xd", [128, D], BF16)
        xdf = sb("xdf", [128, D], BF16)

        sF = [sb(f"sF{i}", [128, D], F32) for i in range(3)]
        sB = [sb(f"sB{i}", [128, D], BF16) for i in range(3)]
        sB.append(sb("sB3", [128, D], F32))
        qF = [sb(f"qF{i}", [128, D], F32) for i in range(2)]
        stt = [sb(f"st{i}", [128, 1], F32) for i in range(7)]
        stf = [sb(f"stf{i}", [128, 1], F32) for i in range(5)]
        st1 = [sb(f"st1_{i}", [128, 1], F32) for i in range(5)]
        bm32 = sb("bm32", [128, 32], F32)
        s32 = sb("s32", [128, 32], F32)

        c_run = psum("c_run", [128, D], F32)
        P = [psum("P0", [128, D], F32), psum("P1", [128, D], F32)]
        tpB = psum("tpB", [128, D], BF16)
        tpB2 = psum("tpB2", [128, D], BF16)
        tp_rot = [0]
        tp_banks = [tpB, tpB2]

        sems = {}
        for name in ["ld", "l2", "l3", "l4", "l5", "gd", "g2", "g3", "g4", "g5",
                     "st", "t", "v", "a", "p"]:
            sems[name] = es.enter_context(nc.semaphore(name))

        def dma(out, in_):
            return lambda e: e.dma_start(out=out, in_=in_)

        t_consts = None
        for dst, src in [(idn[:], idn_h), (tri_i[:], tri_i_h), (tri_s[:], tri_s_h),
                         (cmask[:], mask_h), (ones_r[:], ones_h),
                         (f32s[:], f32s_h),
                         (grow[:], g_h), (brow[:], b_h)]:
            t_consts = S.add("sync", dma(dst, src), deps=[t_consts], inc="ld",
                             inc_n=16, dur=0.5)
        S.const_tok = t_consts

        x_loaded = [None] * NT
        for _tau in range(min(2, NT)):
            x_loaded[_tau] = S.add("sync",
                                   dma(xin[_tau % 2][:],
                                       x_h[_tau * 128:(_tau + 1) * 128, :]),
                                   inc="ld", inc_n=16, dur=3.0)

        cast_toks = {}
        for nm, dst, src, sem in [("k", sc_k, wk_h, "gd"), ("q", sc_q, wq_h, "g2"),
                                  ("v", sc_v, wv_h, "g3"), ("o", sc_o, wo_h, "g4"),
                                  ("g1", sc_g1, wg1_h, "gd"),
                                  ("g2", sc_g2, wg2_h, "g2")]:
            cast_toks[nm] = S.add("gpsimd", dma(dst, src), inc=sem, inc_n=16,
                                  dur=(30.0 if nm in ("k", "q", "v", "o") else 2.0))

        wT_tok = {}
        tsem = {"k": "l2", "q": "l3", "v": "l4", "o": "l5"}
        tsrc = {"k": sc_k, "q": sc_q, "v": sc_v, "o": sc_o}
        tdst = {"k": wkT, "q": wqT, "v": wvT, "o": woT}
        for i in range(8):
            for nm in ["k", "q", "v", "o"]:
                wT_tok[nm] = S.add(
                    "sync",
                    (lambda dst, s: lambda e: e.dma_start_transpose(out=dst, in_=s))(
                        tdst[nm][i][:], tsrc[nm][0:D, i * 128:(i + 1) * 128]),
                    deps=[cast_toks[nm], wT_tok.get(nm)], inc=tsem[nm], inc_n=16,
                    dur=4.0)
        tk = None
        for i in range(8):
            tk = S.add("sync",
                       (lambda i: lambda e: e.dma_start_transpose(
                           out=wg1T[:, i * 128:(i + 1) * 128],
                           in_=sc_g1[0:128, i * 128:(i + 1) * 128]))(i),
                       deps=[cast_toks["g1"], tk], inc="l3", inc_n=16, dur=1.0)
        wT_tok["g1"] = tk
        wT_tok["g2"] = S.add("sync",
                             lambda e: e.dma_start_transpose(out=wg2T[:],
                                                             in_=sc_g2[0:D, 0:128]),
                             deps=[cast_toks["g2"]], inc="l2", inc_n=16, dur=1.0)

        tg1 = S.add("vector", lambda e: e.tensor_copy(out=ghi[:], in_=grow[:]),
                    deps=[t_consts], inc="v", dur=0.2)
        tg2 = S.add("vector", lambda e: e.tensor_tensor(out=glo[:], in0=grow[:],
                                                        in1=ghi[:], op=ALU.subtract),
                    deps=[tg1], inc="v", dur=0.2)
        tb1 = S.add("vector", lambda e: e.tensor_copy(out=bhi[:], in_=brow[:]),
                    deps=[t_consts], inc="v", dur=0.2)
        tb2 = S.add("vector", lambda e: e.tensor_tensor(out=blo[:], in0=brow[:],
                                                        in1=bhi[:], op=ALU.subtract),
                    deps=[tb1], inc="v", dur=0.2)

        def g_mms(e):
            mm = None
            for half in range(2):
                sl = slice(half * 512, (half + 1) * 512)
                mm = e.matmul(out=P[0][:, sl], lhsT=ones_r[:], rhs=ghi[:, sl],
                              start=True, stop=False)
                mm = e.matmul(out=P[0][:, sl], lhsT=ones_r[:], rhs=glo[:, sl],
                              start=False, stop=True)
            return mm

        tgm = S.add("tensor", g_mms, deps=[tg2, t_consts], inc="t")
        tgrep = S.add("scalar", lambda e: e.activation(out=g_rep[:], in_=P[0][:],
                                                       func=AF.Identity),
                      deps=[tgm], inc="a")

        pair_free = [tgrep, Tok("t", 0)]

        def quantize(src_f32, src_tok, dst_bf16, dst_handle, taps=False):
            def tap(name, ap_fn, dep, handle=None):
                if not taps:
                    return
                gtk = S.add("gpsimd", (lambda a=ap_fn, nm=name: (lambda e: e.dma_start(
                    out=dbg["d_q" + nm], in_=a)))(), deps=[dep], inc="gd", inc_n=16)
                if handle is not None:
                    S.buse(handle, gtk)
                return gtk
            """MX-FP6 quantize [128,1024] f32 -> dequantized bf16 in dst.

            Region selectors are captured as int8 tensors so the fp64 shadow
            (which reads int tensors at native values) uses the SAME grid
            region per element as the native pass; the robust floor
            (i - 1[i > z]) then yields the exact fp64 rounding."""
            u, zt, fl = qF[0], qF[1], sF[2]
            au, mfac, minv, corr = sB[0], sB[1], sB[2], sB[3]
            t0 = S.add("vector",
                       lambda e: e.reduce_max(out=bm32[:], in_=src_f32.rearrange(
                           "p (g k) -> p g k", k=32), axis=AX.X,
                           apply_absolute_value=True),
                       deps=[src_tok, S.bfree(bm32)], inc="v")
            S.buse(bm32, t0)
            t1 = S.add("vector",
                       lambda e: e.tensor_scalar(out=s32[:], in0=bm32[:],
                                                 scalar1=1.0 / 7.5, scalar2=1e-30,
                                                 op0=ALU.mult, op1=ALU.max),
                       deps=[t0, S.bfree(s32)], inc="v", dur=0.2)
            t2 = S.add("vector", lambda e: e.reciprocal(out=bm32[:], in_=s32[:]),
                       deps=[t1], inc="v", dur=0.35)
            src3 = src_f32.rearrange("p (g k) -> p g k", k=32)
            inv3 = bm32[:].rearrange("p (g o) -> p g o", o=1).broadcast_to([128, 32, 32])
            s3 = s32[:].rearrange("p (g o) -> p g o", o=1).broadcast_to([128, 32, 32])
            tu = S.add("vector",
                       lambda e: e.tensor_tensor(
                           out=u[:].rearrange("p (g k) -> p g k", k=32),
                           in0=src3, in1=inv3, op=ALU.mult),
                       deps=[t2, src_tok, S.bfree(u)], inc="v")
            S.buse(bm32, tu)
            tau_ = S.add("scalar", lambda e: e.activation(out=au[:], in_=u[:],
                                                          func=AF.Abs),
                         deps=[tu, S.bfree(au)], inc="a")
            ts1 = S.add("vector",
                        lambda e: e.tensor_scalar(out=corr[:], in0=au[:], scalar1=2.0,
                                                  scalar2=-4.0, op0=ALU.is_ge,
                                                  op1=ALU.mult),
                        deps=[tau_, S.bfree(corr)], inc="v", dur=0.6)
            ts2 = S.add("vector",
                        lambda e: e.tensor_scalar(out=minv[:], in0=au[:], scalar1=4.0,
                                                  scalar2=-2.0, op0=ALU.is_ge,
                                                  op1=ALU.mult),
                        deps=[tau_, S.bfree(minv)], inc="v", dur=0.6)
            tm4 = S.add("vector",
                        lambda e: e.scalar_tensor_tensor(out=mfac[:], in0=corr[:],
                                                         scalar=8.0, in1=minv[:],
                                                         op0=ALU.add, op1=ALU.add),
                        deps=[ts1, ts2, S.bfree(mfac)], inc="v")
            tv1 = S.add("vector",
                        lambda e: e.tensor_scalar(out=minv[:], in0=au[:], scalar1=2.0,
                                                  scalar2=0.125, op0=ALU.is_ge,
                                                  op1=ALU.mult),
                        deps=[tau_, tm4], inc="v", dur=0.6)
            tv2 = S.add("vector",
                        lambda e: e.tensor_scalar(out=au[:], in0=au[:], scalar1=4.0,
                                                  scalar2=0.25, op0=ALU.is_ge,
                                                  op1=ALU.mult),
                        deps=[tau_, ts1, ts2, tv1], inc="v", dur=0.6)
            tmv3 = S.add("vector",
                         lambda e: e.scalar_tensor_tensor(out=minv[:], in0=minv[:],
                                                          scalar=0.125, in1=au[:],
                                                          op0=ALU.add, op1=ALU.add),
                         deps=[tv1, tv2], inc="v")
            S.buse(au, tmv3)
            S.buse(corr, tm4)
            ty = S.add("vector",
                       lambda e: e.tensor_tensor(out=zt[:], in0=u[:], in1=mfac[:],
                                                 op=ALU.mult),
                       deps=[tm4, tu, S.bfree(zt)], inc="v")
            S.buse(mfac, ty)
            S.buse(u, ty)
            tw0 = S.add("scalar",
                        lambda e: e.activation(out=u[:], in_=zt[:], func=AF.Identity,
                                               scale=-1.0, bias=16.5),
                        deps=[ty], inc="a")
            prevt = tw0
            prev_tile = u
            seq = [zt, fl, zt, fl, zt, fl]
            used_fl = False
            for bi, bval in enumerate([32.0, 16.0, 8.0, 4.0, 2.0, 1.0]):
                dst = seq[bi]
                extra = []
                if dst is fl and not used_fl:
                    extra = S.bfree(fl)
                    used_fl = True
                tcmp = S.add("vector",
                             (lambda w=prev_tile, b=bval: lambda e: e.tensor_scalar(
                                 out=corr[:], in0=w[:], scalar1=b, scalar2=-b,
                                 op0=ALU.is_ge, op1=ALU.mult))(),
                             deps=[prevt] + (S.bfree(corr) if bi == 0 else []),
                             inc="v", dur=0.6)
                steng = "gpsimd" if bi % 2 == 1 else "vector"
                tstep = S.add(steng,
                              (lambda w=prev_tile, d=dst: lambda e: e.tensor_tensor(
                                  out=d[:], in0=w[:], in1=corr[:], op=ALU.add))(),
                              deps=[tcmp, extra], inc=ENG_SEM[steng],
                              dur=(3.0 if steng == "gpsimd" else 1.25))
                prevt = tstep
                prev_tile = dst
            S.buse(corr, prevt)
            tf2 = S.add("vector",
                        lambda e, wf=prev_tile: e.scalar_tensor_tensor(
                            out=zt[:], in0=wf[:], scalar=16.0, in1=u[:],
                            op0=ALU.add, op1=ALU.subtract),
                        deps=[prevt], inc="v")
            S.buse(u, tf2)
            S.buse(fl, tf2)
            if taps:
                tap("fl", zt[:], tf2, handle=zt)
                tap("mv", minv[:], tmv3, handle=minv)
            tw = S.add("vector",
                       lambda e: e.tensor_tensor(out=u[:], in0=zt[:], in1=minv[:],
                                                 op=ALU.mult),
                       deps=[tf2, tmv3], inc="v")
            S.buse(zt, tw)
            S.buse(minv, tw)
            txd = S.add("vector",
                        lambda e: e.tensor_tensor(
                            out=dst_bf16.rearrange("p (g k) -> p g k", k=32),
                            in0=u[:].rearrange("p (g k) -> p g k", k=32),
                            in1=s3, op=ALU.mult),
                        deps=[tw, t1, S.bfree(dst_handle)], inc="v")
            S.buse(u, txd)
            S.buse(s32, txd)
            return txd

        wq_state = {"tok": None}

        def wquant_stages():
            wd_done = []
            for j in range(8):
                tld = S.add("sync", dma(x2b[2][:], wf_h[j * 128:(j + 1) * 128, :]),
                            deps=[S.bfree(x2b[2])], inc="l5", inc_n=16, dur=3.0)
                tq = quantize(x2b[2][:], tld, xd[:], xd)
                S.buse(x2b[2], tq)
                tst = S.add("gpsimd", dma(sc_wd[j * 128:(j + 1) * 128, :], xd[:]),
                            deps=[tq], inc="g5", inc_n=16, dur=1.0)
                S.buse(xd, tst)
                wd_done.append(tst)
                yield
            tk = None
            for i in range(8):
                tk = S.add("sync",
                           (lambda i: lambda e: e.dma_start_transpose(
                               out=wdT[i][:], in_=sc_wd[0:D, i * 128:(i + 1) * 128]))(i),
                           deps=wd_done + [tk], inc="l4", inc_n=16, dur=4.0)
            wq_state["tok"] = tk

        def transpose8(src_bf16, src_handle, dst_bf16, dst_handle, pair, deps_in):
            tp = tp_banks[tp_rot[0] % 2]
            use_act = True
            tp_rot[0] += 1

            def fn(e, src_bf16=src_bf16, tp=tp):
                mm = None
                for h in range(8):
                    mm = e.transpose(out=tp[:, h * 128:(h + 1) * 128],
                                     in_=src_bf16[:, h * 128:(h + 1) * 128],
                                     identity=idn[:])
                return mm
            tk = S.add("tensor", fn,
                       deps=list(deps_in) + [t_consts, S.bfree(tp)], inc="t", dur=0.6)
            S.buse(src_handle, tk)
            if use_act:
                ev = S.add("scalar",
                           lambda e, dst_bf16=dst_bf16, tp=tp: e.activation(
                               out=dst_bf16, in_=tp[:], func=AF.Identity),
                           deps=[tk, S.bfree(dst_handle)], inc="a", dur=1.2)
            else:
                ev = S.add("vector",
                           lambda e, dst_bf16=dst_bf16, tp=tp: e.tensor_copy(
                               out=dst_bf16, in_=tp[:]),
                           deps=[tk, S.bfree(dst_handle)], inc="v", dur=1.2)
            S.buse(tp, ev)
            return ev

        def proj16(xnf, wT, pair, deps_in):
            def fn(e, xnf=xnf, wT=wT, pair=pair):
                mm = None
                for half in range(2):
                    sl = slice(half * 512, (half + 1) * 512)
                    for i in range(8):
                        mm = e.matmul(out=P[pair][:, sl],
                                      lhsT=xnf[:, i * 128:(i + 1) * 128],
                                      rhs=wT[i][:, sl], start=(i == 0), stop=(i == 7))
                return mm
            return S.add("tensor", fn, deps=list(deps_in) + [pair_free[pair]],
                         inc="t", dur=5.0)

        def ln_apply(src_ap, src_tok, is_x2_add, out_ap, out_handle,
                     extra_dep=None, pool=None):
            sq, stats, lnt = pool
            mN, vN, rN, nN, mmN = stats
            skey = mN
            ta = S.add("scalar",
                       lambda e: e.activation(out=sq[:], in_=src_ap, func=AF.Square,
                                              accum_out=vN[:]),
                       deps=[src_tok, S.bfree(sq), S.bfree(skey)], inc="a", dur=1.3)
            tbv = S.add("scalar",
                        lambda e: e.activation(out=sq[:], in_=src_ap,
                                               func=AF.Identity, accum_out=mN[:]),
                        deps=[src_tok, ta], inc="a", dur=1.3)
            S.buse(sq, tbv)
            tcv = S.add("vector",
                        lambda e: e.tensor_scalar(out=mN[:], in0=mN[:], scalar1=1.0 / D,
                                                  scalar2=None, op0=ALU.mult),
                        deps=[tbv], inc="v", dur=0.15)
            tdv = S.add("vector",
                        lambda e: e.tensor_scalar(out=vN[:], in0=vN[:], scalar1=1.0 / D,
                                                  scalar2=None, op0=ALU.mult),
                        deps=[ta], inc="v", dur=0.15)
            tev = S.add("vector",
                        lambda e: e.tensor_tensor(out=mmN[:], in0=mN[:], in1=mN[:],
                                                  op=ALU.mult),
                        deps=[tcv], inc="v", dur=0.15)
            tfv = S.add("vector",
                        lambda e: e.tensor_tensor(out=vN[:], in0=vN[:], in1=mmN[:],
                                                  op=ALU.subtract),
                        deps=[tdv, tev], inc="v", dur=0.15)
            tgv = S.add("scalar",
                        lambda e: e.activation(out=rN[:], in_=vN[:], func=AF.Ln,
                                               bias=EPS),
                        deps=[tfv], inc="a", dur=0.3)
            thv = S.add("scalar",
                        lambda e: e.activation(out=rN[:], in_=rN[:], func=AF.Exp,
                                               scale=-0.5),
                        deps=[tgv], inc="a", dur=0.3)
            tiv = S.add("vector",
                        lambda e: e.scalar_tensor_tensor(out=nN[:], in0=mN[:],
                                                         scalar=-1.0, in1=rN[:],
                                                         op0=ALU.mult, op1=ALU.mult),
                        deps=[thv, tcv], inc="v", dur=0.15)
            if is_x2_add is None:
                tfin = S.add("scalar",
                             lambda e: e.activation(out=out_ap, in_=src_ap,
                                                    func=AF.Identity, scale=rN[:],
                                                    bias=nN[:]),
                             deps=[thv, tiv, S.bfree(out_handle), extra_dep],
                             inc="a", dur=1.3)
            else:
                x_old_ap, x_old_tok = is_x2_add
                tmid = S.add("scalar",
                             lambda e, lnt=lnt: e.activation(out=lnt[:], in_=src_ap,
                                                             func=AF.Identity,
                                                             scale=rN[:], bias=nN[:]),
                             deps=[thv, tiv, S.bfree(lnt), extra_dep], inc="a",
                             dur=1.3)
                tfin = S.add("gpsimd",
                             lambda e, lnt=lnt: e.tensor_tensor(out=out_ap, in0=lnt[:],
                                                                in1=x_old_ap,
                                                                op=ALU.add),
                             deps=[tmid, x_old_tok, S.bfree(out_handle)], inc="p",
                             dur=3.0)
                S.buse(lnt, tfin)
            S.buse(skey, tfin)
            return tfin

        c_free = Tok("v", 0)
        aps_written = Tok("t", 0)
        out_done = []
        x2tok = {}

        def ffn_stages(ft):
            X2 = x2b[ft % 3]
            X3 = x3b[ft % 2]
            fp = 1 - (ft % 2)
            t_x2 = x2tok[ft]
            ln3_pool = (qF[0], (stf[0], stf[1], stf[2], stf[3], stf[4]), None)
            t_xn3 = ln_apply(X2[:], t_x2, None, xn3[:], xn3, pool=ln3_pool)
            yield
            t_xd = quantize(xn3[:], t_xn3, xd[:], xd)
            S.buse(xn3, t_xd)
            yield
            t_xdf = transpose8(xd[:], xd, xdf[:], xdf, fp, [t_xd])
            tF = proj16(xdf[:], wdT, fp, [t_xdf, wq_state["tok"]])
            S.buse(xdf, tF)

            def bias_fn(e, fp=fp):
                mm = None
                for half in range(2):
                    sl = slice(half * 512, (half + 1) * 512)
                    mm = e.matmul(out=P[fp][:, sl], lhsT=ones_r[:], rhs=bhi[:, sl],
                                  start=False, stop=False, skip_group_check=True)
                    mm = e.matmul(out=P[fp][:, sl], lhsT=ones_r[:], rhs=blo[:, sl],
                                  start=False, stop=True, skip_group_check=True)
                return mm
            tFb = S.add("tensor", bias_fn, deps=[tF, tb2, t_consts], inc="t")
            ln4_pool = (qF[0], (stf[0], stf[1], stf[2], stf[3], stf[4]), qF[1])
            t_x3 = ln_apply(P[fp][:], tFb, (X2[:], t_x2), X3[:], X3, pool=ln4_pool)
            pair_free[fp] = t_x3
            S.buse(X2, t_x3)
            t_out = S.add("gpsimd", dma(out_h[ft * 128:(ft + 1) * 128, :], X3[:]),
                          deps=[t_x3], inc="st", inc_n=16, dur=3.0)
            S.buse(X3, t_out)
            out_done.append(t_out)


        def attn_stages(tau):
            nonlocal c_free, aps_written
            X = xin[tau % 2]
            X2 = x2b[tau % 3]
            pr = tau % 2

            ln1_pool = (sF[0], (st1[0], st1[1], st1[2], st1[3], st1[4]), None)
            t_xn1 = ln_apply(X[:], x_loaded[tau], None, xn1tok[:], xn1tok,
                             pool=ln1_pool)
            t_xn1f = transpose8(xn1tok[:], xn1tok, xn1f[:], xn1f, pr, [t_xn1])

            yield
            tK = proj16(xn1f[:], wkT, pr, [t_xn1f, wT_tok["k"], S.bfree(etok)])
            t_e = S.add("scalar",
                        lambda e, pp=P[pr]: e.activation(out=etok[:], in_=pp[:],
                                                         func=AF.Exp),
                        deps=[tK], inc="a")
            pair_free[pr] = t_e

            def c_incl(e, tau=tau):
                mm = None
                for half in range(2):
                    sl = slice(half * 512, (half + 1) * 512)
                    mm = e.matmul(out=c_run[:, sl], lhsT=tri_i[:], rhs=etok[:, sl],
                                  start=(tau == 0), stop=False,
                                  skip_group_check=True)
                return mm
            tci = S.add("tensor", c_incl, deps=[t_e, c_free, t_consts], inc="t")
            t_r0 = S.add("scalar",
                         lambda e: e.activation(out=r_sb[:], in_=c_run[:],
                                                func=AF.Ln),
                         deps=[tci, S.bfree(r_sb)], inc="a")
            t_r = S.add("scalar",
                        lambda e: e.activation(out=r_sb[:], in_=r_sb[:],
                                               func=AF.Exp, scale=-1.0),
                        deps=[t_r0], inc="a")
            c_free = t_r0


            yield
            tQ = proj16(xn1f[:], wqT, 1 - pr, [t_xn1f, wT_tok["q"]])
            silq = sF[1]
            t_sq = S.add("scalar",
                         lambda e, pp=P[1 - pr]: e.activation(out=silq[:], in_=pp[:],
                                                              func=AF.Silu),
                         deps=[tQ, S.bfree(silq)], inc="a", tag="silu")
            pair_free[1 - pr] = t_sq
            t_qt = S.add("gpsimd",
                         lambda e: e.tensor_tensor(out=qtok[:], in0=silq[:],
                                                   in1=r_sb[:], op=ALU.mult),
                         deps=[t_sq, t_r, S.bfree(qtok)], inc="p", dur=3.0)
            S.buse(silq, t_qt)
            S.buse(r_sb, t_qt)
            def c_strict(e):
                mm = None
                for half in range(2):
                    sl = slice(half * 512, (half + 1) * 512)
                    mm = e.matmul(out=c_run[:, sl], lhsT=tri_s[:], rhs=etok[:, sl],
                                  start=False, stop=True, skip_group_check=True)
                return mm
            tcs = S.add("tensor", c_strict, deps=[t_r], inc="t")

            yield
            tV = proj16(xn1f[:], wvT, pr, [t_xn1f, wT_tok["v"], t_e, S.bfree(vtok)])
            t_v = S.add("scalar",
                        lambda e, pp=P[pr]: e.activation(out=vtok[:], in_=pp[:],
                                                         func=AF.Identity),
                        deps=[tV], inc="a")
            pair_free[pr] = t_v

            def g1_fn(e, pr=pr):
                mm = None
                for i in range(8):
                    mm = e.matmul(out=P[1 - pr][:, 0:128],
                                  lhsT=wg1T[:, i * 128:(i + 1) * 128],
                                  rhs=xn1f[:, i * 128:(i + 1) * 128],
                                  start=(i == 0), stop=(i == 7))
                return mm
            tG1 = S.add("tensor", g1_fn,
                        deps=[t_xn1f, wT_tok["g1"], pair_free[1 - pr]], inc="t")
            t_g1 = S.add("vector",
                         lambda e, pp=P[1 - pr]: e.tensor_copy(out=g1f[:],
                                                               in_=pp[:, 0:128]),
                         deps=[tG1, S.bfree(g1f)], inc="v")
            pair_free[1 - pr] = t_g1

            def g2_fn(e, pr=pr):
                mm = None
                for half in range(2):
                    sl = slice(half * 512, (half + 1) * 512)
                    mm = e.matmul(out=P[1 - pr][:, sl], lhsT=g1f[:], rhs=wg2T[:, sl],
                                  start=True, stop=True)
                return mm
            tG2 = S.add("tensor", g2_fn, deps=[t_g1, wT_tok["g2"], pair_free[1 - pr]],
                        inc="t")
            S.buse(g1f, tG2)
            t_sg = S.add("scalar",
                         lambda e, pp=P[1 - pr]: e.activation(out=sg[:], in_=pp[:],
                                                              func=AF.Silu),
                         deps=[tG2, S.bfree(sg)], inc="a", tag="silu")
            pair_free[1 - pr] = t_sg

            yield
            t_ef = transpose8(etok[:], None, efeat[:], efeat, pr, [t_e, tcs])
            t_qf = transpose8(qtok[:], qtok, qfeat[:], qfeat, 1 - pr, [t_qt])

            if tau > 0:
                t_abf = S.add("scalar",
                              lambda e: e.activation(out=a_bf[:], in_=a_sb[:],
                                                     func=AF.Identity, scale=SCALE),
                              deps=[aps_written, S.bfree(a_bf)], inc="a")
            else:
                t_abf = None

            def a_upd(e, pr=pr):
                mm = None
                for h in range(8):
                    sl = slice(h * 128, (h + 1) * 128)
                    mm = e.matmul(out=P[pr][:, sl], lhsT=etok[:, sl], rhs=vtok[:, sl],
                                  start=(h % 4 == 0), stop=(h % 4 == 3))
                return mm
            tAu = S.add("tensor", a_upd, deps=[t_e, t_v, pair_free[pr]], inc="t")
            S.buse(etok, tAu)
            if tau > 0:
                tAa = S.add("vector",
                            lambda e, pp=P[pr]: e.tensor_tensor(
                                out=a_sb[:], in0=pp[:], in1=a_sb[:], op=ALU.add),
                            deps=[tAu, t_abf], inc="v")
            else:
                tAa = S.add("vector",
                            lambda e, pp=P[pr]: e.tensor_copy(out=a_sb[:], in_=pp[:]),
                            deps=[tAu], inc="v")
            pair_free[pr] = tAa
            aps_written = tAa

            yield
            def wr_fn(e, pr=pr):
                mm = None
                for h in range(8):
                    sl = slice(h * 128, (h + 1) * 128)
                    mm = e.matmul(out=P[pr][:, sl], lhsT=efeat[:, sl],
                                  rhs=qfeat[:, sl], start=True, stop=True)
                return mm
            tW = S.add("tensor", wr_fn, deps=[t_ef, t_qf, pair_free[pr]], inc="t")
            S.buse(efeat, tW)
            mask3 = cmask[:].rearrange("p (o n) -> p o n", o=1).broadcast_to(
                [128, 8, 128])
            t_wm = S.add("vector",
                         lambda e, pp=P[pr]: e.tensor_tensor(
                             out=wmask[:].rearrange("p (h n) -> p h n", n=128),
                             in0=pp[:].rearrange("p (h n) -> p h n", n=128),
                             in1=mask3, op=ALU.mult),
                         deps=[tW, S.bfree(wmask)], inc="v")
            pair_free[pr] = t_wm

            def o_fn(e, pr=pr, first=(tau == 0)):
                mm = None
                for h in range(8):
                    sl = slice(h * 128, (h + 1) * 128)
                    mm = e.matmul(out=P[1 - pr][:, sl], lhsT=wmask[:, sl],
                                  rhs=vtok[:, sl], start=True, stop=first)
                    if not first:
                        mm = e.matmul(out=P[1 - pr][:, sl], lhsT=qfeat[:, sl],
                                      rhs=a_bf[:, sl], start=False, stop=True)
                return mm
            tO = S.add("tensor", o_fn,
                       deps=[t_wm, t_abf, t_qf, t_v, pair_free[1 - pr]], inc="t",
                       dur=2.9)
            S.buse(wmask, tO)
            S.buse(qfeat, tO)
            S.buse(vtok, tO)
            S.buse(a_bf, tO)

            o_scr = sF[0]
            v2, r2 = stt[5], stt[0]
            t_osq = S.add("scalar",
                          lambda e, pp=P[1 - pr]: e.activation(out=o_scr[:], in_=pp[:],
                                                               func=AF.Square,
                                                               accum_out=v2[:]),
                          deps=[tO, S.bfree(o_scr), S.bfree(v2)], inc="a")
            S.buse(o_scr, t_osq)
            t_or1 = S.add("scalar",
                          lambda e: e.activation(out=r2[:], in_=v2[:], func=AF.Ln,
                                                 scale=1.0 / D, bias=EPS),
                          deps=[t_osq], inc="a", dur=0.3)
            t_or2 = S.add("scalar",
                          lambda e: e.activation(out=r2[:], in_=r2[:], func=AF.Exp,
                                                 scale=-0.5),
                          deps=[t_or1], inc="a", dur=0.3)
            o_n2 = sF[0]
            t_on = S.add("vector",
                         lambda e, pp=P[1 - pr]: e.scalar_tensor_tensor(
                             out=o_n2[:], in0=pp[:], scalar=r2[:], in1=g_rep[:],
                             op0=ALU.mult, op1=ALU.mult),
                         deps=[t_or2, tgrep, S.bfree(o_n2)], inc="v")
            S.buse(v2, t_on)
            pair_free[1 - pr] = t_on
            t_o2 = S.add("gpsimd",
                         lambda e: e.tensor_tensor(out=o2t[:], in0=o_n2[:], in1=sg[:],
                                                   op=ALU.mult),
                         deps=[t_on, t_sg, S.bfree(o2t)], inc="p")
            S.buse(o_n2, t_o2)
            S.buse(sg, t_o2)

            yield
            t_o2f = transpose8(o2t[:], o2t, o2f[:], o2f, pr, [t_o2])
            tAO = proj16(o2f[:], woT, pr, [t_o2f, wT_tok["o"]])
            S.buse(o2f, tAO)
            ln2_pool = (sF[0], (stt[1], stt[2], stt[3], stt[4], stt[6]), sF[1])
            t_x2 = ln_apply(P[pr][:], tAO, (X[:], x_loaded[tau]), X2[:], X2,
                            pool=ln2_pool)
            pair_free[pr] = t_x2
            S.buse(X, t_x2)
            x2tok[tau] = t_x2

            if tau + 2 < NT:
                x_loaded[tau + 2] = S.add("sync",
                                          dma(xin[(tau + 2) % 2][:],
                                              x_h[(tau + 2) * 128:(tau + 3) * 128, :]),
                                          deps=[t_x2], inc="ld", inc_n=16, dur=3.0)

        wq = wquant_stages()
        for tau in range(NT):
            gens = [attn_stages(tau)]
            if tau == 0:
                gens.append(wq)
            elif tau == 1:
                gens.append(wq)
            if tau >= 2:
                gens.append(ffn_stages(tau - 2))
            while gens:
                for g in list(gens):
                    try:
                        next(g)
                    except StopIteration:
                        gens.remove(g)
        for _ in ffn_stages(NT - 2):
            pass
        for _ in ffn_stages(NT - 1):
            pass

        S.add("gpsimd", lambda e: e.engine_nop(), deps=out_done)

        S.reorder()

        with nc.Block() as block:
            def emit_for(engname):
                lastw = {}

                def run(eng):
                    for ent in S.entries:
                        ename, fn, deps, incs = ent[0], ent[1], ent[2], ent[3]
                        if ename != engname:
                            continue
                        for d in deps:
                            if lastw.get(d.sem, 0) < d.cnt:
                                eng.wait_ge(sems[d.sem], d.cnt)
                                lastw[d.sem] = d.cnt
                        for semname, n, base in incs:
                            if semname in DMA_SEMS and base > 0:
                                if lastw.get(semname, 0) < base:
                                    eng.wait_ge(sems[semname], base)
                                    lastw[semname] = base
                        instr = fn(eng)
                        for semname, n, base in incs:
                            instr.then_inc(sems[semname], n)
                return run

            block.sync(emit_for("sync"))
            block.gpsimd(emit_for("gpsimd"))
            block.vector(emit_for("vector"))
            block.scalar(emit_for("scalar"))
            block.tensor(emit_for("tensor"))

    return nc


def _consts():
    s = np.arange(128)[:, None]
    t = np.arange(128)[None, :]
    tri_i = (s <= t).astype(ml_dtypes.bfloat16)
    tri_s = (s > t).astype(ml_dtypes.bfloat16)
    mask = ((s <= t) * SCALE).astype(np.float32)
    idn = np.eye(128, dtype=ml_dtypes.bfloat16)
    ones = np.ones((1, 128), dtype=ml_dtypes.bfloat16)
    cvals = np.array(CVALS, dtype=np.float32)
    f32s = np.tile(cvals[None, :], (128, 1)).astype(np.float32)
    return idn, tri_i, tri_s, mask, ones, f32s


_NC_CACHE = {}


def kernel(L0_attn_hidden_states, L0_attn_Wq, L0_attn_Wk, L0_attn_Wv, L0_attn_Wg1,
           L0_attn_Wg2, L0_attn_g_norm_weight, L0_attn_Wo, L0_ffn_weight, L0_ffn_bias,
           _debug=False):
    x = np.ascontiguousarray(np.asarray(L0_attn_hidden_states, dtype=np.float32))
    idn, tri_i, tri_s, mask, ones, f32s = _consts()
    common = {
        "Wq": np.ascontiguousarray(np.asarray(L0_attn_Wq, np.float32)),
        "Wk": np.ascontiguousarray(np.asarray(L0_attn_Wk, np.float32)),
        "Wv": np.ascontiguousarray(np.asarray(L0_attn_Wv, np.float32)),
        "Wo": np.ascontiguousarray(np.asarray(L0_attn_Wo, np.float32)),
        "Wf": np.ascontiguousarray(np.asarray(L0_ffn_weight, np.float32)),
        "Wg1": np.ascontiguousarray(np.asarray(L0_attn_Wg1, np.float32)),
        "Wg2": np.ascontiguousarray(np.asarray(L0_attn_Wg2, np.float32)),
        "gnw": np.ascontiguousarray(np.asarray(L0_attn_g_norm_weight,
                                               np.float32).reshape(1, D)),
        "fbias": np.ascontiguousarray(np.asarray(L0_ffn_bias,
                                                 np.float32).reshape(1, D)),
        "c_idn": idn, "c_tri_i": tri_i, "c_tri_s": tri_s, "c_mask": mask,
        "c_ones": ones, "c_f32s": f32s,
    }
    import os
    dbg_tau = int(os.environ.get("KDBG_TAU", "0"))
    key = (bool(_debug), dbg_tau)
    if key not in _NC_CACHE:
        _NC_CACHE[key] = build_nc(debug=_debug, dbg_tau=dbg_tau)
    nc = _NC_CACHE[key]
    in_maps = []
    for b in range(N_CORES):
        m = dict(common)
        m["x"] = np.ascontiguousarray(x[b])
        in_maps.append(m)
    res = run_bass_kernel_spmd(nc, in_maps, core_ids=list(range(N_CORES)))
    out = np.stack([res.results[b]["out"] for b in range(N_CORES)], axis=0)
    if _debug:
        return out, res.results
    return out


# version 27
# speedup vs baseline: 1.0482x; 1.0375x over previous
import numpy as np
import ml_dtypes
from contextlib import ExitStack

import concourse.bass as bass
import concourse.mybir as mybir
from concourse.bass_utils import run_bass_kernel_spmd

F32 = mybir.dt.float32
BF16 = mybir.dt.bfloat16
I32 = mybir.dt.int32
AF = mybir.ActivationFunctionType
ALU = mybir.AluOpType
AX = mybir.AxisListType

T_TOK = 2048
D = 1024
NT = 16
SCALE = float(128) ** -0.5
EPS = 1e-5
N_CORES = 8
CVALS = [0.5, EPS, -0.5, -1.0, SCALE, 1.0 / D, 16.5, 0.0]

DMA_SEMS = {"ld", "l2", "l3", "l4", "l5", "gd", "g2", "g3", "g4", "g5", "st"}
ENG_SEM = {"tensor": "t", "vector": "v", "scalar": "a", "gpsimd": "p"}


class Tok:
    __slots__ = ("sem", "cnt")

    def __init__(self, sem, cnt):
        self.sem, self.cnt = sem, cnt


ENG_DUR = {"vector": 0.9, "scalar": 1.2, "tensor": 1.0, "gpsimd": 3.0, "sync": 3.0}


class Sched:
    def __init__(self):
        self.entries = []
        self.cnt = {}
        self.buf_free = {}
        self.const_tok = None

    def add(self, eng, fn, deps=(), inc=None, inc_n=1, dur=None, tag=None):
        flat = []
        for d in deps:
            if d is None:
                continue
            if isinstance(d, list):
                flat.extend(x for x in d if x is not None)
            else:
                flat.append(d)
        if eng == "scalar" and self.const_tok is not None:
            flat.append(self.const_tok)
        deps = flat
        incs = []
        tok = None
        if inc is not None:
            base = self.cnt.get(inc, 0)
            self.cnt[inc] = base + inc_n
            tok = Tok(inc, self.cnt[inc])
            incs.append((inc, inc_n, base))
        self.entries.append([eng, fn, deps, incs, tok, dur, tag])
        return tok

    def bfree(self, buf):
        return list(self.buf_free.get(id(buf), []))

    def buse(self, buf, tok):
        self.buf_free.setdefault(id(buf), []).append(tok)

    def reorder(self):
        ents = self.entries
        n = len(ents)
        prod = {}
        for i, ent in enumerate(ents):
            if ent[4] is not None:
                prod[id(ent[4])] = i
        preds = [set() for _ in range(n)]
        last_t = None
        for i, ent in enumerate(ents):
            for d in ent[2]:
                j = prod.get(id(d))
                if j is not None and j != i:
                    preds[i].add(j)
            if ent[0] == "tensor":
                if last_t is not None:
                    preds[i].add(last_t)
                last_t = i
        succs = [[] for _ in range(n)]
        for i in range(n):
            for j in preds[i]:
                succs[j].append(i)
        dur = [ent[5] if ent[5] is not None else ENG_DUR[ent[0]] for ent in ents]
        prio = [0.0] * n
        for i in range(n - 1, -1, -1):
            m = 0.0
            for s in succs[i]:
                if prio[s] > m:
                    m = prio[s]
            prio[i] = dur[i] + m
        indeg = [len(preds[i]) for i in range(n)]
        finish = [0.0] * n
        eng_free = {}
        sem_busy = {}
        ready = [i for i in range(n) if indeg[i] == 0]
        order = []
        cur_tab = "nat"
        while ready:
            best = None
            for i in ready:
                ent = ents[i]
                st = eng_free.get(ent[0], 0.0)
                for j in preds[i]:
                    if finish[j] > st:
                        st = finish[j]
                isdma = bool(ent[3]) and ent[3][0][0] in DMA_SEMS
                if isdma:
                    sb_ = sem_busy.get(ent[3][0][0], 0.0)
                    if sb_ > st:
                        st = sb_
                pen = 0.0
                if ent[0] == "scalar" and (ent[6] or "nat") != cur_tab:
                    pen = 1.3
                key = (st + pen, -prio[i], i)
                if best is None or key < best[0]:
                    best = (key, i, st + pen, isdma)
            _, i, st, isdma = best
            ready.remove(i)
            ent = ents[i]
            finish[i] = st + dur[i]
            if isdma:
                sem_busy[ent[3][0][0]] = finish[i]
                eng_free[ent[0]] = st + 0.05
            else:
                eng_free[ent[0]] = finish[i]
                if ent[0] == "scalar":
                    cur_tab = ent[6] or "nat"
            order.append(i)
            for s in succs[i]:
                indeg[s] -= 1
                if indeg[s] == 0:
                    ready.append(s)
        assert len(order) == n, "cycle in schedule graph"
        self.entries = [ents[i] for i in order]
        cnt = {}
        for ent in self.entries:
            newincs = []
            for (sem, nn, _b) in ent[3]:
                b = cnt.get(sem, 0)
                cnt[sem] = b + nn
                newincs.append((sem, nn, b))
                if ent[4] is not None and ent[4].sem == sem:
                    ent[4].cnt = b + nn
            ent[3][:] = newincs
        return max(finish) if n else 0.0


def build_nc(debug=False, dbg_tau=0):
    nc = bass.Bass("TRN2", target_bir_lowering=False)

    x_h = nc.dram_tensor("x", [T_TOK, D], F32, kind="ExternalInput").ap()
    wq_h = nc.dram_tensor("Wq", [D, D], F32, kind="ExternalInput").ap()
    wk_h = nc.dram_tensor("Wk", [D, D], F32, kind="ExternalInput").ap()
    wv_h = nc.dram_tensor("Wv", [D, D], F32, kind="ExternalInput").ap()
    wo_h = nc.dram_tensor("Wo", [D, D], F32, kind="ExternalInput").ap()
    wf_h = nc.dram_tensor("Wf", [D, D], F32, kind="ExternalInput").ap()
    wg1_h = nc.dram_tensor("Wg1", [128, D], F32, kind="ExternalInput").ap()
    wg2_h = nc.dram_tensor("Wg2", [D, 128], F32, kind="ExternalInput").ap()
    g_h = nc.dram_tensor("gnw", [1, D], F32, kind="ExternalInput").ap()
    b_h = nc.dram_tensor("fbias", [1, D], F32, kind="ExternalInput").ap()
    idn_h = nc.dram_tensor("c_idn", [128, 128], BF16, kind="ExternalInput").ap()
    tri_i_h = nc.dram_tensor("c_tri_i", [128, 128], BF16, kind="ExternalInput").ap()
    tri_s_h = nc.dram_tensor("c_tri_s", [128, 128], BF16, kind="ExternalInput").ap()
    mask_h = nc.dram_tensor("c_mask", [128, 128], F32, kind="ExternalInput").ap()
    ones_h = nc.dram_tensor("c_ones", [1, 128], BF16, kind="ExternalInput").ap()
    f32s_h = nc.dram_tensor("c_f32s", [128, 8], F32, kind="ExternalInput").ap()
    out_h = nc.dram_tensor("out", [T_TOK, D], F32, kind="ExternalOutput").ap()

    sc_q = nc.dram_tensor("sc_q", [D, D], BF16, kind="Internal").ap()
    sc_k = nc.dram_tensor("sc_k", [D, D], BF16, kind="Internal").ap()
    sc_v = nc.dram_tensor("sc_v", [D, D], BF16, kind="Internal").ap()
    sc_o = nc.dram_tensor("sc_o", [D, D], BF16, kind="Internal").ap()
    sc_g1 = nc.dram_tensor("sc_g1", [128, D], BF16, kind="Internal").ap()
    sc_g2 = nc.dram_tensor("sc_g2", [D, 128], BF16, kind="Internal").ap()
    sc_wd = nc.dram_tensor("sc_wd", [D, D], BF16, kind="Internal").ap()

    dbg = {}
    if debug:
        for name in ["d_etok", "d_r", "d_qtok", "d_vtok", "d_o2", "d_x2", "d_xn3",
                     "d_xd", "d_wd", "d_xn1", "d_wk0", "d_xn1f",
                     "d_abf", "d_wm", "d_opre", "d_ef", "d_qf",
                     "d_qi", "d_qd", "d_qc", "d_qm", "d_qmv", "d_qu", "d_qfl",
                     "d_fps", "d_x3", "d_xdf"]:
            dbg[name] = nc.dram_tensor(name, [128, D], F32, kind="ExternalOutput").ap()

    es = ExitStack()
    S = Sched()

    def sb(name, shape, dt):
        return es.enter_context(nc.sbuf_tensor(name, shape, dt))

    def psum(name, shape, dt):
        return es.enter_context(nc.psum_tensor(name, shape, dt))

    with es:
        wkT = [sb(f"wkT{i}", [128, D], BF16) for i in range(8)]
        wqT = [sb(f"wqT{i}", [128, D], BF16) for i in range(8)]
        wvT = [sb(f"wvT{i}", [128, D], BF16) for i in range(8)]
        woT = [sb(f"woT{i}", [128, D], BF16) for i in range(8)]
        wdT = [sb(f"wdT{i}", [128, D], BF16) for i in range(8)]
        wg1T = sb("wg1T", [128, 8 * 128], BF16)
        wg2T = sb("wg2T", [128, D], BF16)
        g_rep = sb("g_rep", [128, D], F32)
        grow = sb("grow", [1, D], F32)
        ghi = sb("ghi", [1, D], BF16)
        glo = sb("glo", [1, D], BF16)
        brow = sb("brow", [1, D], F32)
        bhi = sb("bhi", [1, D], BF16)
        blo = sb("blo", [1, D], BF16)
        idn = sb("idn", [128, 128], BF16)
        tri_i = sb("tri_i", [128, 128], BF16)
        tri_s = sb("tri_s", [128, 128], BF16)
        cmask = sb("cmask", [128, 128], F32)
        ones_r = sb("ones_r", [1, 128], BF16)
        f32s = sb("f32s", [128, 8], F32)
        for _ci, _cv in enumerate(CVALS):
            nc.const_aps.register(F32, float(_cv), f32s[:, _ci:_ci + 1])

        xin = [sb(f"xin{i}", [128, D], F32) for i in range(2)]
        x2b = [sb(f"x2b{i}", [128, D], F32) for i in range(3)]
        x3b = [sb(f"x3b{i}", [128, D], F32) for i in range(2)]

        xn1tok = sb("xn1tok", [128, D], BF16)
        xn1f = sb("xn1f", [128, D], BF16)
        etok = sb("etok", [128, D], BF16)
        efeat = sb("efeat", [128, D], BF16)
        qtok = sb("qtok", [128, D], BF16)
        qfeat = sb("qfeat", [128, D], BF16)
        vtok = sb("vtok", [128, D], BF16)
        g1f = sb("g1f", [128, 128], BF16)
        r_sb = sb("r_sb", [128, D], F32)
        sg = sb("sg", [128, D], F32)
        wmask = sb("wmask", [128, D], BF16)
        a_bf = sb("a_bf", [128, D], BF16)
        a_sb = sb("a_sb", [128, D], F32)
        o2t = sb("o2t", [128, D], BF16)
        o2f = sb("o2f", [128, D], BF16)
        xn3 = sb("xn3", [128, D], F32)
        xd = sb("xd", [128, D], BF16)
        xdf = sb("xdf", [128, D], BF16)

        sF = [sb(f"sF{i}", [128, D], F32) for i in range(3)]
        sB = [sb(f"sB{i}", [128, D], BF16) for i in range(3)]
        sB.append(sb("sB3", [128, D], F32))
        qF = [sb(f"qF{i}", [128, D], F32) for i in range(2)]
        stt = [sb(f"st{i}", [128, 1], F32) for i in range(7)]
        stf = [sb(f"stf{i}", [128, 1], F32) for i in range(5)]
        st1 = [sb(f"st1_{i}", [128, 1], F32) for i in range(5)]
        bm32 = sb("bm32", [128, 32], F32)
        s32a = [sb("s32", [128, 32], F32), sb("s32b", [128, 32], F32)]
        qcnt = [0]

        c_run = psum("c_run", [128, D], F32)
        P = [psum("P0", [128, D], F32), psum("P1", [128, D], F32)]
        tpB = psum("tpB", [128, D], BF16)
        tpB2 = psum("tpB2", [128, D], BF16)
        tp_rot = [0]
        tp_banks = [tpB, tpB2]

        sems = {}
        for name in ["ld", "l2", "l3", "l4", "l5", "gd", "g2", "g3", "g4", "g5",
                     "st", "t", "v", "a", "p"]:
            sems[name] = es.enter_context(nc.semaphore(name))

        def dma(out, in_):
            return lambda e: e.dma_start(out=out, in_=in_)

        t_consts = None
        for dst, src in [(idn[:], idn_h), (tri_i[:], tri_i_h), (tri_s[:], tri_s_h),
                         (cmask[:], mask_h), (ones_r[:], ones_h),
                         (f32s[:], f32s_h),
                         (grow[:], g_h), (brow[:], b_h)]:
            t_consts = S.add("sync", dma(dst, src), deps=[t_consts], inc="ld",
                             inc_n=16, dur=0.5)
        S.const_tok = t_consts

        x_loaded = [None] * NT
        for _tau in range(min(2, NT)):
            x_loaded[_tau] = S.add("sync",
                                   dma(xin[_tau % 2][:],
                                       x_h[_tau * 128:(_tau + 1) * 128, :]),
                                   inc="ld", inc_n=16, dur=3.0)

        cast_toks = {}
        for nm, dst, src, sem in [("k", sc_k, wk_h, "gd"), ("q", sc_q, wq_h, "g2"),
                                  ("v", sc_v, wv_h, "g3"), ("o", sc_o, wo_h, "g4"),
                                  ("g1", sc_g1, wg1_h, "gd"),
                                  ("g2", sc_g2, wg2_h, "g2")]:
            cast_toks[nm] = S.add("gpsimd", dma(dst, src), inc=sem, inc_n=16,
                                  dur=(30.0 if nm in ("k", "q", "v", "o") else 2.0))

        wT_tok = {}
        tsem = {"k": "l2", "q": "l3", "v": "l4", "o": "l5"}
        tsrc = {"k": sc_k, "q": sc_q, "v": sc_v, "o": sc_o}
        tdst = {"k": wkT, "q": wqT, "v": wvT, "o": woT}
        for i in range(8):
            for nm in ["k", "q", "v", "o"]:
                wT_tok[nm] = S.add(
                    "sync",
                    (lambda dst, s: lambda e: e.dma_start_transpose(out=dst, in_=s))(
                        tdst[nm][i][:], tsrc[nm][0:D, i * 128:(i + 1) * 128]),
                    deps=[cast_toks[nm], wT_tok.get(nm)], inc=tsem[nm], inc_n=16,
                    dur=4.0)
        tk = None
        for i in range(8):
            tk = S.add("sync",
                       (lambda i: lambda e: e.dma_start_transpose(
                           out=wg1T[:, i * 128:(i + 1) * 128],
                           in_=sc_g1[0:128, i * 128:(i + 1) * 128]))(i),
                       deps=[cast_toks["g1"], tk], inc="l3", inc_n=16, dur=1.0)
        wT_tok["g1"] = tk
        wT_tok["g2"] = S.add("sync",
                             lambda e: e.dma_start_transpose(out=wg2T[:],
                                                             in_=sc_g2[0:D, 0:128]),
                             deps=[cast_toks["g2"]], inc="l2", inc_n=16, dur=1.0)

        tg1 = S.add("vector", lambda e: e.tensor_copy(out=ghi[:], in_=grow[:]),
                    deps=[t_consts], inc="v", dur=0.2)
        tg2 = S.add("vector", lambda e: e.tensor_tensor(out=glo[:], in0=grow[:],
                                                        in1=ghi[:], op=ALU.subtract),
                    deps=[tg1], inc="v", dur=0.2)
        tb1 = S.add("vector", lambda e: e.tensor_copy(out=bhi[:], in_=brow[:]),
                    deps=[t_consts], inc="v", dur=0.2)
        tb2 = S.add("vector", lambda e: e.tensor_tensor(out=blo[:], in0=brow[:],
                                                        in1=bhi[:], op=ALU.subtract),
                    deps=[tb1], inc="v", dur=0.2)

        def g_mms(e):
            mm = None
            for half in range(2):
                sl = slice(half * 512, (half + 1) * 512)
                mm = e.matmul(out=P[0][:, sl], lhsT=ones_r[:], rhs=ghi[:, sl],
                              start=True, stop=False)
                mm = e.matmul(out=P[0][:, sl], lhsT=ones_r[:], rhs=glo[:, sl],
                              start=False, stop=True)
            return mm

        tgm = S.add("tensor", g_mms, deps=[tg2, t_consts], inc="t")
        tgrep = S.add("scalar", lambda e: e.activation(out=g_rep[:], in_=P[0][:],
                                                       func=AF.Identity),
                      deps=[tgm], inc="a")

        pair_free = [tgrep, Tok("t", 0)]

        def quantize(src_f32, src_tok, dst_bf16, dst_handle, taps=False):
            def tap(name, ap_fn, dep, handle=None):
                if not taps:
                    return
                gtk = S.add("gpsimd", (lambda a=ap_fn, nm=name: (lambda e: e.dma_start(
                    out=dbg["d_q" + nm], in_=a)))(), deps=[dep], inc="gd", inc_n=16)
                if handle is not None:
                    S.buse(handle, gtk)
                return gtk
            """MX-FP6 quantize [128,1024] f32 -> dequantized bf16 in dst.

            Region selectors are captured as int8 tensors so the fp64 shadow
            (which reads int tensors at native values) uses the SAME grid
            region per element as the native pass; the robust floor
            (i - 1[i > z]) then yields the exact fp64 rounding."""
            u, zt, fl = qF[0], qF[1], sF[2]
            au, mfac, minv, corr = sB[0], sB[1], sB[2], sB[3]
            s32 = s32a[qcnt[0] % 2]
            qcnt[0] += 1
            t0 = S.add("vector",
                       lambda e: e.reduce_max(out=bm32[:], in_=src_f32.rearrange(
                           "p (g k) -> p g k", k=32), axis=AX.X,
                           apply_absolute_value=True),
                       deps=[src_tok, S.bfree(bm32)], inc="v")
            S.buse(bm32, t0)
            t1 = S.add("vector",
                       lambda e: e.tensor_scalar(out=s32[:], in0=bm32[:],
                                                 scalar1=1.0 / 7.5, scalar2=1e-30,
                                                 op0=ALU.mult, op1=ALU.max),
                       deps=[t0, S.bfree(s32)], inc="v", dur=0.2)
            t2 = S.add("vector", lambda e: e.reciprocal(out=bm32[:], in_=s32[:]),
                       deps=[t1], inc="v", dur=0.35)
            src3 = src_f32.rearrange("p (g k) -> p g k", k=32)
            inv3 = bm32[:].rearrange("p (g o) -> p g o", o=1).broadcast_to([128, 32, 32])
            s3 = s32[:].rearrange("p (g o) -> p g o", o=1).broadcast_to([128, 32, 32])
            tu = S.add("vector",
                       lambda e: e.tensor_tensor(
                           out=u[:].rearrange("p (g k) -> p g k", k=32),
                           in0=src3, in1=inv3, op=ALU.mult),
                       deps=[t2, src_tok, S.bfree(u)], inc="v")
            S.buse(bm32, tu)
            tau_ = S.add("scalar", lambda e: e.activation(out=au[:], in_=u[:],
                                                          func=AF.Abs),
                         deps=[tu, S.bfree(au)], inc="a")
            ts1 = S.add("vector",
                        lambda e: e.tensor_scalar(out=corr[:], in0=au[:], scalar1=2.0,
                                                  scalar2=-4.0, op0=ALU.is_ge,
                                                  op1=ALU.mult),
                        deps=[tau_, S.bfree(corr)], inc="v", dur=0.6)
            ts2 = S.add("vector",
                        lambda e: e.tensor_scalar(out=minv[:], in0=au[:], scalar1=4.0,
                                                  scalar2=-2.0, op0=ALU.is_ge,
                                                  op1=ALU.mult),
                        deps=[tau_, S.bfree(minv)], inc="v", dur=0.6)
            tm4 = S.add("vector",
                        lambda e: e.scalar_tensor_tensor(out=mfac[:], in0=corr[:],
                                                         scalar=8.0, in1=minv[:],
                                                         op0=ALU.add, op1=ALU.add),
                        deps=[ts1, ts2, S.bfree(mfac)], inc="v")
            tv1 = S.add("vector",
                        lambda e: e.tensor_scalar(out=minv[:], in0=au[:], scalar1=2.0,
                                                  scalar2=0.125, op0=ALU.is_ge,
                                                  op1=ALU.mult),
                        deps=[tau_, tm4], inc="v", dur=0.6)
            tv2 = S.add("vector",
                        lambda e: e.tensor_scalar(out=au[:], in0=au[:], scalar1=4.0,
                                                  scalar2=0.25, op0=ALU.is_ge,
                                                  op1=ALU.mult),
                        deps=[tau_, ts1, ts2, tv1], inc="v", dur=0.6)
            tmv3 = S.add("vector",
                         lambda e: e.scalar_tensor_tensor(out=minv[:], in0=minv[:],
                                                          scalar=0.125, in1=au[:],
                                                          op0=ALU.add, op1=ALU.add),
                         deps=[tv1, tv2], inc="v")
            S.buse(au, tmv3)
            S.buse(corr, tm4)
            ty = S.add("vector",
                       lambda e: e.tensor_tensor(out=zt[:], in0=u[:], in1=mfac[:],
                                                 op=ALU.mult),
                       deps=[tm4, tu, S.bfree(zt)], inc="v")
            S.buse(mfac, ty)
            S.buse(u, ty)
            tw0 = S.add("scalar",
                        lambda e: e.activation(out=u[:], in_=zt[:], func=AF.Identity,
                                               scale=-1.0, bias=16.5),
                        deps=[ty], inc="a")
            prevt = tw0
            prev_tile = u
            seq = [zt, fl, zt, fl, zt, fl]
            used_fl = False
            for bi, bval in enumerate([32.0, 16.0, 8.0, 4.0, 2.0, 1.0]):
                dst = seq[bi]
                extra = []
                if dst is fl and not used_fl:
                    extra = S.bfree(fl)
                    used_fl = True
                tcmp = S.add("vector",
                             (lambda w=prev_tile, b=bval: lambda e: e.tensor_scalar(
                                 out=corr[:], in0=w[:], scalar1=b, scalar2=-b,
                                 op0=ALU.is_ge, op1=ALU.mult))(),
                             deps=[prevt] + (S.bfree(corr) if bi == 0 else []),
                             inc="v", dur=0.6)
                steng = "gpsimd" if bi % 2 == 1 else "vector"
                tstep = S.add(steng,
                              (lambda w=prev_tile, d=dst: lambda e: e.tensor_tensor(
                                  out=d[:], in0=w[:], in1=corr[:], op=ALU.add))(),
                              deps=[tcmp, extra], inc=ENG_SEM[steng],
                              dur=(3.0 if steng == "gpsimd" else 1.25))
                prevt = tstep
                prev_tile = dst
            S.buse(corr, prevt)
            tf2 = S.add("vector",
                        lambda e, wf=prev_tile: e.scalar_tensor_tensor(
                            out=zt[:], in0=wf[:], scalar=16.0, in1=u[:],
                            op0=ALU.add, op1=ALU.subtract),
                        deps=[prevt], inc="v")
            S.buse(u, tf2)
            S.buse(fl, tf2)
            if taps:
                tap("fl", zt[:], tf2, handle=zt)
                tap("mv", minv[:], tmv3, handle=minv)
            tw = S.add("vector",
                       lambda e: e.tensor_tensor(out=fl[:], in0=zt[:], in1=minv[:],
                                                 op=ALU.mult),
                       deps=[tf2, tmv3], inc="v")
            S.buse(zt, tw)
            S.buse(minv, tw)
            txd = S.add("vector",
                        lambda e: e.tensor_tensor(
                            out=dst_bf16.rearrange("p (g k) -> p g k", k=32),
                            in0=fl[:].rearrange("p (g k) -> p g k", k=32),
                            in1=s3, op=ALU.mult),
                        deps=[tw, t1, S.bfree(dst_handle)], inc="v")
            S.buse(fl, txd)
            S.buse(s32, txd)
            return txd

        wq_state = {"tok": None}

        def wquant_stages():
            wd_done = []
            for j in range(8):
                tld = S.add("sync", dma(x2b[2][:], wf_h[j * 128:(j + 1) * 128, :]),
                            deps=[S.bfree(x2b[2])], inc="l5", inc_n=16, dur=3.0)
                tq = quantize(x2b[2][:], tld, xd[:], xd)
                S.buse(x2b[2], tq)
                tst = S.add("gpsimd", dma(sc_wd[j * 128:(j + 1) * 128, :], xd[:]),
                            deps=[tq], inc="g5", inc_n=16, dur=1.0)
                S.buse(xd, tst)
                wd_done.append(tst)
                yield
            tk = None
            for i in range(8):
                tk = S.add("sync",
                           (lambda i: lambda e: e.dma_start_transpose(
                               out=wdT[i][:], in_=sc_wd[0:D, i * 128:(i + 1) * 128]))(i),
                           deps=wd_done + [tk], inc="l4", inc_n=16, dur=4.0)
            wq_state["tok"] = tk

        def transpose8(src_bf16, src_handle, dst_bf16, dst_handle, pair, deps_in):
            tp = tp_banks[tp_rot[0] % 2]
            use_act = (tp_rot[0] % 2 == 1)
            tp_rot[0] += 1

            def fn(e, src_bf16=src_bf16, tp=tp):
                mm = None
                for h in range(8):
                    mm = e.transpose(out=tp[:, h * 128:(h + 1) * 128],
                                     in_=src_bf16[:, h * 128:(h + 1) * 128],
                                     identity=idn[:])
                return mm
            tk = S.add("tensor", fn,
                       deps=list(deps_in) + [t_consts, S.bfree(tp)], inc="t", dur=0.6)
            S.buse(src_handle, tk)
            if use_act:
                ev = S.add("scalar",
                           lambda e, dst_bf16=dst_bf16, tp=tp: e.activation(
                               out=dst_bf16, in_=tp[:], func=AF.Identity),
                           deps=[tk, S.bfree(dst_handle)], inc="a", dur=1.2)
            else:
                ev = S.add("vector",
                           lambda e, dst_bf16=dst_bf16, tp=tp: e.tensor_copy(
                               out=dst_bf16, in_=tp[:]),
                           deps=[tk, S.bfree(dst_handle)], inc="v", dur=1.2)
            S.buse(tp, ev)
            return ev

        def proj16(xnf, wT, pair, deps_in):
            def fn(e, xnf=xnf, wT=wT, pair=pair):
                mm = None
                for half in range(2):
                    sl = slice(half * 512, (half + 1) * 512)
                    for i in range(8):
                        mm = e.matmul(out=P[pair][:, sl],
                                      lhsT=xnf[:, i * 128:(i + 1) * 128],
                                      rhs=wT[i][:, sl], start=(i == 0), stop=(i == 7))
                return mm
            return S.add("tensor", fn, deps=list(deps_in) + [pair_free[pair]],
                         inc="t", dur=5.0)

        def ln_apply(src_ap, src_tok, is_x2_add, out_ap, out_handle,
                     extra_dep=None, pool=None):
            sq, stats, lnt = pool
            mN, vN, rN, nN, mmN = stats
            skey = mN
            ta = S.add("scalar",
                       lambda e: e.activation(out=sq[:], in_=src_ap, func=AF.Square,
                                              accum_out=vN[:]),
                       deps=[src_tok, S.bfree(sq), S.bfree(skey)], inc="a", dur=1.3)
            tbv = S.add("scalar",
                        lambda e: e.activation(out=sq[:], in_=src_ap,
                                               func=AF.Identity, accum_out=mN[:]),
                        deps=[src_tok, ta], inc="a", dur=1.3)
            S.buse(sq, tbv)
            tcv = S.add("vector",
                        lambda e: e.tensor_scalar(out=mN[:], in0=mN[:], scalar1=1.0 / D,
                                                  scalar2=None, op0=ALU.mult),
                        deps=[tbv], inc="v", dur=0.15)
            tdv = S.add("vector",
                        lambda e: e.tensor_scalar(out=vN[:], in0=vN[:], scalar1=1.0 / D,
                                                  scalar2=None, op0=ALU.mult),
                        deps=[ta], inc="v", dur=0.15)
            tev = S.add("vector",
                        lambda e: e.tensor_tensor(out=mmN[:], in0=mN[:], in1=mN[:],
                                                  op=ALU.mult),
                        deps=[tcv], inc="v", dur=0.15)
            tfv = S.add("vector",
                        lambda e: e.tensor_tensor(out=vN[:], in0=vN[:], in1=mmN[:],
                                                  op=ALU.subtract),
                        deps=[tdv, tev], inc="v", dur=0.15)
            tgv = S.add("scalar",
                        lambda e: e.activation(out=rN[:], in_=vN[:], func=AF.Ln,
                                               bias=EPS),
                        deps=[tfv], inc="a", dur=0.3)
            thv = S.add("scalar",
                        lambda e: e.activation(out=rN[:], in_=rN[:], func=AF.Exp,
                                               scale=-0.5),
                        deps=[tgv], inc="a", dur=0.3)
            tiv = S.add("vector",
                        lambda e: e.scalar_tensor_tensor(out=nN[:], in0=mN[:],
                                                         scalar=-1.0, in1=rN[:],
                                                         op0=ALU.mult, op1=ALU.mult),
                        deps=[thv, tcv], inc="v", dur=0.15)
            if is_x2_add is None:
                tfin = S.add("scalar",
                             lambda e: e.activation(out=out_ap, in_=src_ap,
                                                    func=AF.Identity, scale=rN[:],
                                                    bias=nN[:]),
                             deps=[thv, tiv, S.bfree(out_handle), extra_dep],
                             inc="a", dur=1.3)
            else:
                x_old_ap, x_old_tok = is_x2_add
                tmid = S.add("scalar",
                             lambda e, lnt=lnt: e.activation(out=lnt[:], in_=src_ap,
                                                             func=AF.Identity,
                                                             scale=rN[:], bias=nN[:]),
                             deps=[thv, tiv, S.bfree(lnt), extra_dep], inc="a",
                             dur=1.3)
                tfin = S.add("gpsimd",
                             lambda e, lnt=lnt: e.tensor_tensor(out=out_ap, in0=lnt[:],
                                                                in1=x_old_ap,
                                                                op=ALU.add),
                             deps=[tmid, x_old_tok, S.bfree(out_handle)], inc="p",
                             dur=3.0)
                S.buse(lnt, tfin)
            S.buse(skey, tfin)
            return tfin

        c_free = Tok("v", 0)
        aps_written = Tok("t", 0)
        out_done = []
        x2tok = {}

        def ffn_stages(ft):
            X2 = x2b[ft % 3]
            X3 = x3b[ft % 2]
            fp = 1 - (ft % 2)
            t_x2 = x2tok[ft]
            ln3_pool = (qF[0], (stf[0], stf[1], stf[2], stf[3], stf[4]), None)
            t_xn3 = ln_apply(X2[:], t_x2, None, xn3[:], xn3, pool=ln3_pool)
            yield
            t_xd = quantize(xn3[:], t_xn3, xd[:], xd)
            S.buse(xn3, t_xd)
            yield
            t_xdf = transpose8(xd[:], xd, xdf[:], xdf, fp, [t_xd])
            tF = proj16(xdf[:], wdT, fp, [t_xdf, wq_state["tok"]])
            S.buse(xdf, tF)

            def bias_fn(e, fp=fp):
                mm = None
                for half in range(2):
                    sl = slice(half * 512, (half + 1) * 512)
                    mm = e.matmul(out=P[fp][:, sl], lhsT=ones_r[:], rhs=bhi[:, sl],
                                  start=False, stop=False, skip_group_check=True)
                    mm = e.matmul(out=P[fp][:, sl], lhsT=ones_r[:], rhs=blo[:, sl],
                                  start=False, stop=True, skip_group_check=True)
                return mm
            tFb = S.add("tensor", bias_fn, deps=[tF, tb2, t_consts], inc="t")
            ln4_pool = (qF[0], (stf[0], stf[1], stf[2], stf[3], stf[4]), qF[1])
            t_x3 = ln_apply(P[fp][:], tFb, (X2[:], t_x2), X3[:], X3, pool=ln4_pool)
            pair_free[fp] = t_x3
            S.buse(X2, t_x3)
            t_out = S.add("gpsimd", dma(out_h[ft * 128:(ft + 1) * 128, :], X3[:]),
                          deps=[t_x3], inc="st", inc_n=16, dur=3.0)
            S.buse(X3, t_out)
            out_done.append(t_out)


        def attn_stages(tau):
            nonlocal c_free, aps_written
            X = xin[tau % 2]
            X2 = x2b[tau % 3]
            pr = tau % 2

            ln1_pool = (sF[0], (st1[0], st1[1], st1[2], st1[3], st1[4]), None)
            t_xn1 = ln_apply(X[:], x_loaded[tau], None, xn1tok[:], xn1tok,
                             pool=ln1_pool)
            t_xn1f = transpose8(xn1tok[:], xn1tok, xn1f[:], xn1f, pr, [t_xn1])

            yield
            tK = proj16(xn1f[:], wkT, pr, [t_xn1f, wT_tok["k"], S.bfree(etok)])
            t_e = S.add("scalar",
                        lambda e, pp=P[pr]: e.activation(out=etok[:], in_=pp[:],
                                                         func=AF.Exp),
                        deps=[tK], inc="a")
            pair_free[pr] = t_e

            def c_incl(e, tau=tau):
                mm = None
                for half in range(2):
                    sl = slice(half * 512, (half + 1) * 512)
                    mm = e.matmul(out=c_run[:, sl], lhsT=tri_i[:], rhs=etok[:, sl],
                                  start=(tau == 0), stop=False,
                                  skip_group_check=True)
                return mm
            tci = S.add("tensor", c_incl, deps=[t_e, c_free, t_consts], inc="t")
            t_r0 = S.add("scalar",
                         lambda e: e.activation(out=r_sb[:], in_=c_run[:],
                                                func=AF.Ln),
                         deps=[tci, S.bfree(r_sb)], inc="a")
            t_r = S.add("scalar",
                        lambda e: e.activation(out=r_sb[:], in_=r_sb[:],
                                               func=AF.Exp, scale=-1.0),
                        deps=[t_r0], inc="a")
            c_free = t_r0


            yield
            tQ = proj16(xn1f[:], wqT, 1 - pr, [t_xn1f, wT_tok["q"]])
            silq = sF[1]
            t_sq = S.add("scalar",
                         lambda e, pp=P[1 - pr]: e.activation(out=silq[:], in_=pp[:],
                                                              func=AF.Silu),
                         deps=[tQ, S.bfree(silq)], inc="a", tag="silu")
            pair_free[1 - pr] = t_sq
            t_qt = S.add("gpsimd",
                         lambda e: e.tensor_tensor(out=qtok[:], in0=silq[:],
                                                   in1=r_sb[:], op=ALU.mult),
                         deps=[t_sq, t_r, S.bfree(qtok)], inc="p", dur=3.0)
            S.buse(silq, t_qt)
            S.buse(r_sb, t_qt)
            def c_strict(e):
                mm = None
                for half in range(2):
                    sl = slice(half * 512, (half + 1) * 512)
                    mm = e.matmul(out=c_run[:, sl], lhsT=tri_s[:], rhs=etok[:, sl],
                                  start=False, stop=True, skip_group_check=True)
                return mm
            tcs = S.add("tensor", c_strict, deps=[t_r], inc="t")

            yield
            tV = proj16(xn1f[:], wvT, pr, [t_xn1f, wT_tok["v"], t_e, S.bfree(vtok)])
            t_v = S.add("scalar",
                        lambda e, pp=P[pr]: e.activation(out=vtok[:], in_=pp[:],
                                                         func=AF.Identity),
                        deps=[tV], inc="a")
            pair_free[pr] = t_v

            def g1_fn(e, pr=pr):
                mm = None
                for i in range(8):
                    mm = e.matmul(out=P[1 - pr][:, 0:128],
                                  lhsT=wg1T[:, i * 128:(i + 1) * 128],
                                  rhs=xn1f[:, i * 128:(i + 1) * 128],
                                  start=(i == 0), stop=(i == 7))
                return mm
            tG1 = S.add("tensor", g1_fn,
                        deps=[t_xn1f, wT_tok["g1"], pair_free[1 - pr]], inc="t")
            t_g1 = S.add("vector",
                         lambda e, pp=P[1 - pr]: e.tensor_copy(out=g1f[:],
                                                               in_=pp[:, 0:128]),
                         deps=[tG1, S.bfree(g1f)], inc="v")
            pair_free[1 - pr] = t_g1

            def g2_fn(e, pr=pr):
                mm = None
                for half in range(2):
                    sl = slice(half * 512, (half + 1) * 512)
                    mm = e.matmul(out=P[1 - pr][:, sl], lhsT=g1f[:], rhs=wg2T[:, sl],
                                  start=True, stop=True)
                return mm
            tG2 = S.add("tensor", g2_fn, deps=[t_g1, wT_tok["g2"], pair_free[1 - pr]],
                        inc="t")
            S.buse(g1f, tG2)
            t_sg = S.add("scalar",
                         lambda e, pp=P[1 - pr]: e.activation(out=sg[:], in_=pp[:],
                                                              func=AF.Silu),
                         deps=[tG2, S.bfree(sg)], inc="a", tag="silu")
            pair_free[1 - pr] = t_sg

            yield
            t_ef = transpose8(etok[:], None, efeat[:], efeat, pr, [t_e, tcs])
            t_qf = transpose8(qtok[:], qtok, qfeat[:], qfeat, 1 - pr, [t_qt])

            if tau > 0:
                t_abf = S.add("scalar",
                              lambda e: e.activation(out=a_bf[:], in_=a_sb[:],
                                                     func=AF.Identity, scale=SCALE),
                              deps=[aps_written, S.bfree(a_bf)], inc="a")
            else:
                t_abf = None

            def a_upd(e, pr=pr):
                mm = None
                for h in range(8):
                    sl = slice(h * 128, (h + 1) * 128)
                    mm = e.matmul(out=P[pr][:, sl], lhsT=etok[:, sl], rhs=vtok[:, sl],
                                  start=(h % 4 == 0), stop=(h % 4 == 3))
                return mm
            tAu = S.add("tensor", a_upd, deps=[t_e, t_v, pair_free[pr]], inc="t")
            S.buse(etok, tAu)
            if tau > 0:
                tAa = S.add("vector",
                            lambda e, pp=P[pr]: e.tensor_tensor(
                                out=a_sb[:], in0=pp[:], in1=a_sb[:], op=ALU.add),
                            deps=[tAu, t_abf], inc="v")
            else:
                tAa = S.add("vector",
                            lambda e, pp=P[pr]: e.tensor_copy(out=a_sb[:], in_=pp[:]),
                            deps=[tAu], inc="v")
            pair_free[pr] = tAa
            aps_written = tAa

            yield
            def wr_fn(e, pr=pr):
                mm = None
                for h in range(8):
                    sl = slice(h * 128, (h + 1) * 128)
                    mm = e.matmul(out=P[pr][:, sl], lhsT=efeat[:, sl],
                                  rhs=qfeat[:, sl], start=True, stop=True)
                return mm
            tW = S.add("tensor", wr_fn, deps=[t_ef, t_qf, pair_free[pr]], inc="t")
            S.buse(efeat, tW)
            mask3 = cmask[:].rearrange("p (o n) -> p o n", o=1).broadcast_to(
                [128, 8, 128])
            t_wm = S.add("vector",
                         lambda e, pp=P[pr]: e.tensor_tensor(
                             out=wmask[:].rearrange("p (h n) -> p h n", n=128),
                             in0=pp[:].rearrange("p (h n) -> p h n", n=128),
                             in1=mask3, op=ALU.mult),
                         deps=[tW, S.bfree(wmask)], inc="v")
            pair_free[pr] = t_wm

            def o_fn(e, pr=pr, first=(tau == 0)):
                mm = None
                for h in range(8):
                    sl = slice(h * 128, (h + 1) * 128)
                    mm = e.matmul(out=P[1 - pr][:, sl], lhsT=wmask[:, sl],
                                  rhs=vtok[:, sl], start=True, stop=first)
                    if not first:
                        mm = e.matmul(out=P[1 - pr][:, sl], lhsT=qfeat[:, sl],
                                      rhs=a_bf[:, sl], start=False, stop=True)
                return mm
            tO = S.add("tensor", o_fn,
                       deps=[t_wm, t_abf, t_qf, t_v, pair_free[1 - pr]], inc="t",
                       dur=2.9)
            S.buse(wmask, tO)
            S.buse(qfeat, tO)
            S.buse(vtok, tO)
            S.buse(a_bf, tO)

            o_scr = sF[0]
            v2, r2 = stt[5], stt[0]
            t_osq = S.add("scalar",
                          lambda e, pp=P[1 - pr]: e.activation(out=o_scr[:], in_=pp[:],
                                                               func=AF.Square,
                                                               accum_out=v2[:]),
                          deps=[tO, S.bfree(o_scr), S.bfree(v2)], inc="a")
            S.buse(o_scr, t_osq)
            t_or1 = S.add("scalar",
                          lambda e: e.activation(out=r2[:], in_=v2[:], func=AF.Ln,
                                                 scale=1.0 / D, bias=EPS),
                          deps=[t_osq], inc="a", dur=0.3)
            t_or2 = S.add("scalar",
                          lambda e: e.activation(out=r2[:], in_=r2[:], func=AF.Exp,
                                                 scale=-0.5),
                          deps=[t_or1], inc="a", dur=0.3)
            o_n2 = sF[0]
            t_on = S.add("vector",
                         lambda e, pp=P[1 - pr]: e.scalar_tensor_tensor(
                             out=o_n2[:], in0=pp[:], scalar=r2[:], in1=g_rep[:],
                             op0=ALU.mult, op1=ALU.mult),
                         deps=[t_or2, tgrep, S.bfree(o_n2)], inc="v")
            S.buse(v2, t_on)
            pair_free[1 - pr] = t_on
            t_o2 = S.add("gpsimd",
                         lambda e: e.tensor_tensor(out=o2t[:], in0=o_n2[:], in1=sg[:],
                                                   op=ALU.mult),
                         deps=[t_on, t_sg, S.bfree(o2t)], inc="p")
            S.buse(o_n2, t_o2)
            S.buse(sg, t_o2)

            yield
            t_o2f = transpose8(o2t[:], o2t, o2f[:], o2f, pr, [t_o2])
            tAO = proj16(o2f[:], woT, pr, [t_o2f, wT_tok["o"]])
            S.buse(o2f, tAO)
            ln2_pool = (sF[0], (stt[1], stt[2], stt[3], stt[4], stt[6]), sF[1])
            t_x2 = ln_apply(P[pr][:], tAO, (X[:], x_loaded[tau]), X2[:], X2,
                            pool=ln2_pool)
            pair_free[pr] = t_x2
            S.buse(X, t_x2)
            x2tok[tau] = t_x2

            if tau + 2 < NT:
                x_loaded[tau + 2] = S.add("sync",
                                          dma(xin[(tau + 2) % 2][:],
                                              x_h[(tau + 2) * 128:(tau + 3) * 128, :]),
                                          deps=[t_x2], inc="ld", inc_n=16, dur=3.0)

        wq = wquant_stages()
        for tau in range(NT):
            gens = [attn_stages(tau)]
            if tau == 0:
                gens.append(wq)
            elif tau == 1:
                gens.append(wq)
            if tau >= 2:
                gens.append(ffn_stages(tau - 2))
            while gens:
                for g in list(gens):
                    try:
                        next(g)
                    except StopIteration:
                        gens.remove(g)
        for _ in ffn_stages(NT - 2):
            pass
        for _ in ffn_stages(NT - 1):
            pass

        S.add("gpsimd", lambda e: e.engine_nop(), deps=out_done)

        S.reorder()

        with nc.Block() as block:
            def emit_for(engname):
                lastw = {}

                def run(eng):
                    for ent in S.entries:
                        ename, fn, deps, incs = ent[0], ent[1], ent[2], ent[3]
                        if ename != engname:
                            continue
                        for d in deps:
                            if lastw.get(d.sem, 0) < d.cnt:
                                eng.wait_ge(sems[d.sem], d.cnt)
                                lastw[d.sem] = d.cnt
                        for semname, n, base in incs:
                            if semname in DMA_SEMS and base > 0:
                                if lastw.get(semname, 0) < base:
                                    eng.wait_ge(sems[semname], base)
                                    lastw[semname] = base
                        instr = fn(eng)
                        for semname, n, base in incs:
                            instr.then_inc(sems[semname], n)
                return run

            block.sync(emit_for("sync"))
            block.gpsimd(emit_for("gpsimd"))
            block.vector(emit_for("vector"))
            block.scalar(emit_for("scalar"))
            block.tensor(emit_for("tensor"))

    return nc


def _consts():
    s = np.arange(128)[:, None]
    t = np.arange(128)[None, :]
    tri_i = (s <= t).astype(ml_dtypes.bfloat16)
    tri_s = (s > t).astype(ml_dtypes.bfloat16)
    mask = ((s <= t) * SCALE).astype(np.float32)
    idn = np.eye(128, dtype=ml_dtypes.bfloat16)
    ones = np.ones((1, 128), dtype=ml_dtypes.bfloat16)
    cvals = np.array(CVALS, dtype=np.float32)
    f32s = np.tile(cvals[None, :], (128, 1)).astype(np.float32)
    return idn, tri_i, tri_s, mask, ones, f32s


_NC_CACHE = {}


def kernel(L0_attn_hidden_states, L0_attn_Wq, L0_attn_Wk, L0_attn_Wv, L0_attn_Wg1,
           L0_attn_Wg2, L0_attn_g_norm_weight, L0_attn_Wo, L0_ffn_weight, L0_ffn_bias,
           _debug=False):
    x = np.ascontiguousarray(np.asarray(L0_attn_hidden_states, dtype=np.float32))
    idn, tri_i, tri_s, mask, ones, f32s = _consts()
    common = {
        "Wq": np.ascontiguousarray(np.asarray(L0_attn_Wq, np.float32)),
        "Wk": np.ascontiguousarray(np.asarray(L0_attn_Wk, np.float32)),
        "Wv": np.ascontiguousarray(np.asarray(L0_attn_Wv, np.float32)),
        "Wo": np.ascontiguousarray(np.asarray(L0_attn_Wo, np.float32)),
        "Wf": np.ascontiguousarray(np.asarray(L0_ffn_weight, np.float32)),
        "Wg1": np.ascontiguousarray(np.asarray(L0_attn_Wg1, np.float32)),
        "Wg2": np.ascontiguousarray(np.asarray(L0_attn_Wg2, np.float32)),
        "gnw": np.ascontiguousarray(np.asarray(L0_attn_g_norm_weight,
                                               np.float32).reshape(1, D)),
        "fbias": np.ascontiguousarray(np.asarray(L0_ffn_bias,
                                                 np.float32).reshape(1, D)),
        "c_idn": idn, "c_tri_i": tri_i, "c_tri_s": tri_s, "c_mask": mask,
        "c_ones": ones, "c_f32s": f32s,
    }
    import os
    dbg_tau = int(os.environ.get("KDBG_TAU", "0"))
    key = (bool(_debug), dbg_tau)
    if key not in _NC_CACHE:
        _NC_CACHE[key] = build_nc(debug=_debug, dbg_tau=dbg_tau)
    nc = _NC_CACHE[key]
    in_maps = []
    for b in range(N_CORES):
        m = dict(common)
        m["x"] = np.ascontiguousarray(x[b])
        in_maps.append(m)
    res = run_bass_kernel_spmd(nc, in_maps, core_ids=list(range(N_CORES)))
    out = np.stack([res.results[b]["out"] for b in range(N_CORES)], axis=0)
    if _debug:
        return out, res.results
    return out
